# version 01
# baseline (speedup 1.0000x reference)
import contextlib
import numpy as np

import concourse.bass as bass
import concourse.mybir as mybir
from concourse.bass_utils import run_bass_kernel_spmd

F32 = mybir.dt.float32
BF16 = mybir.dt.bfloat16
AF = mybir.ActivationFunctionType
ALU = mybir.AluOpType
AX = mybir.AxisListType.X

D = 1024
T = 4096
NH = 16
HD = 64
KT = 512
NKT = T // KT
QT = 512
HID = 4096
NCH = 8
TTF = 256
NFT = T // TTF
W = TTF + 2
ATT0 = T - QT

C23 = 12582912.0
C52 = 6755399441055744.0
BIG = 3.0e38
EPS_LN = 1e-5
SCALE = 0.125


class Op:
    __slots__ = ("eng", "fn", "deps", "is_dma", "tiny", "seq", "inc_val", "milestone")

    def __init__(self, eng, fn, deps, is_dma, tiny):
        self.eng, self.fn, self.deps, self.is_dma, self.tiny = eng, fn, deps, is_dma, tiny
        self.seq = -1
        self.inc_val = 0
        self.milestone = None


class Sched:
    ENGS = ("sync", "gpsimd", "vector", "scalar", "tensor")

    def __init__(self):
        self.ops = []
        self.gate = []

    def add(self, eng, fn, deps=(), dma=False, tiny=False):
        op = Op(eng, fn, [d for d in deps if d is not None] + self.gate, dma, tiny)
        op.seq = len(self.ops)
        self.ops.append(op)
        return op

    def barrier(self, eng, deps):
        return self.add(eng, None, deps)

    def phase_gate(self):
        tails = {}
        dma_idx = {}
        for op in self.ops:
            if op.is_dma:
                i = dma_idx.get(op.eng, 0)
                dma_idx[op.eng] = i + 1
                tails[(op.eng, "D%d" % (i % self.RR))] = op
            else:
                tails[(op.eng, "")] = op
        self.gate = []
        joins = [self.barrier(e, list(tails.values())) for e in self.ENGS]
        self.gate = joins
        return joins

    RR = 8

    def emit(self, nc, sems):
        needed = set()
        for op in self.ops:
            for d in op.deps:
                needed.add(d.seq)
        cnt = {}
        dma_idx = {}
        for op in self.ops:
            if op.is_dma:
                i = dma_idx.get(op.eng, 0)
                dma_idx[op.eng] = i + 1
                op.inc_val = 16
                op.milestone = ("%sD%d" % (op.eng, i % self.RR), 16 * (i // self.RR + 1))
            elif op.seq in needed:
                op.inc_val = 1
                c = cnt.get(op.eng, 0) + 1
                cnt[op.eng] = c
                op.milestone = (op.eng, c)
        per_eng = {e: [o for o in self.ops if o.eng == e] for e in self.ENGS}

        def emit_engine(eng_obj, eng_name):
            water = {}
            recent = []
            for op in per_eng[eng_name]:
                for d in op.deps:
                    if d.milestone is None:
                        raise RuntimeError("dep lacks milestone")
                    ch, val = d.milestone
                    if val > water.get(ch, 0):
                        water[ch] = val
                        eng_obj.wait_ge(sems[ch], val)
                if op.is_dma:
                    ch, val = op.milestone
                    if val > 16 and val - 16 > water.get(ch, 0):
                        water[ch] = val - 16
                        eng_obj.wait_ge(sems[ch], val - 16)
                if op.fn is not None:
                    inst = op.fn(eng_obj)
                    if op.milestone is not None:
                        ch, _ = op.milestone
                        inst.then_inc(sems[ch], op.inc_val)
                    recent.append(op)
                    if len(recent) > 3:
                        recent.pop(0)
                elif op.milestone is not None:
                    ch, _ = op.milestone
                    eng_obj.nop().then_inc(sems[ch], op.inc_val)

        with nc.Block() as blk:
            blk.sync(lambda e: emit_engine(e, "sync"))
            blk.gpsimd(lambda e: emit_engine(e, "gpsimd"))
            blk.vector(lambda e: emit_engine(e, "vector"))
            blk.scalar(lambda e: emit_engine(e, "scalar"))
            blk.tensor(lambda e: emit_engine(e, "tensor"))


def dram_ap(t, off, dims):
    return bass.AP(t, off, [list(d) for d in dims])


def build(debug=False):
    nc = bass.Bass("TRN2", target_bir_lowering=False)

    xTh = nc.dram_tensor("xT", [D, T], F32, kind="ExternalInput")
    wqTh = nc.dram_tensor("wqT", [D, D], F32, kind="ExternalInput")
    wkTh = nc.dram_tensor("wkT", [D, D], F32, kind="ExternalInput")
    wvTh = nc.dram_tensor("wvT", [D, D], F32, kind="ExternalInput")
    qkb = nc.dram_tensor("qkb", [128, 16], F32, kind="ExternalInput").ap()
    vbrow = nc.dram_tensor("vbrow", [1, D], F32, kind="ExternalInput").ap()
    projTh = nc.dram_tensor("projT", [D, D], F32, kind="ExternalInput")
    pjb = nc.dram_tensor("pjb", [128, 8], F32, kind="ExternalInput").ap()
    w1Th = nc.dram_tensor("w1T", [D, HID], F32, kind="ExternalInput")
    cw = nc.dram_tensor("cw", [128, 96], F32, kind="ExternalInput").ap()
    cbi = nc.dram_tensor("cbi", [128, 32], F32, kind="ExternalInput").ap()
    w2Th = nc.dram_tensor("w2T", [HID, D], F32, kind="ExternalInput")
    outTh = nc.dram_tensor("outT", [D, T], F32, kind="ExternalOutput")

    k_hbmh = nc.dram_tensor("k_hbm", [NH * 80, T], BF16, kind="Internal")
    k_hbm = k_hbmh.ap()
    v_hbmh = nc.dram_tensor("v_hbm", [T, D], BF16, kind="Internal")
    v_hbm = v_hbmh.ap()
    xattnh = nc.dram_tensor("xattn", [D, QT], F32, kind="Internal")

    dbg = {}
    if debug:
        for nm, shp, dt in [("dbg_xh", [128, 8 * KT], BF16),
                            ("dbg_lhist", [128, 576], F32),
                            ("dbg_mhist", [128, 576], F32),
                            ("dbg_b", [16, NH * QT], BF16),
                            ("dbg_ofm", [128, 8 * QT], BF16),
                            ("dbg_x1", [128, 8 * QT], F32),
                            ("dbg_n1", [128, 8 * W], BF16),
                            ("dbg_u", [128, 32 * TTF], F32)]:
            dbg[nm] = nc.dram_tensor(nm, shp, dt, kind="ExternalOutput").ap()

    S = Sched()
    st = contextlib.ExitStack()
    with st:
        def sb(name, shape, dt=F32, stk=st):
            return stk.enter_context(nc.sbuf_tensor(name, list(shape), dt))

        sems = {}
        for e in Sched.ENGS:
            sems[e] = st.enter_context(nc.semaphore(f"sem_{e}"))
            for i in range(Sched.RR):
                sems["%sD%d" % (e, i)] = st.enter_context(
                    nc.semaphore(f"sem_{e}D{i}"))

        onescol_b = sb("onescol_b", [128, 1], BF16)
        ones1_b = sb("ones1_b", [1, 128], BF16)
        ones1_f = sb("ones1_f", [1, 128], F32)
        onescol_f = sb("onescol_f", [128, 1], F32)
        identf = sb("identf", [128, 128], F32)
        identb = sb("identb", [128, 128], BF16)
        Ca = sb("Ca", [128, 1], F32)
        Ca1 = sb("Ca1", [1, 1], F32)
        qkb_s = sb("qkb_s", [128, 16], F32)
        qb8 = sb("qb8", [128, 8], F32)
        pjb_s = sb("pjb_s", [128, 8], F32)
        cw_s = sb("cw_s", [128, 96], F32)
        cb_s = sb("cb_s", [128, 32], F32)
        rowb = sb("rowb", [1, 2560], F32)
        c1_s = sb("c1_s", [1, 1], F32)
        c2_s = sb("c2_s", [1, 1], F32)
        rc1 = sb("rc1", [1, 1], F32)
        rc2 = sb("rc2", [1, 1], F32)
        sw1 = sb("sw1", [128, 1], F32)
        sw2 = sb("sw2", [128, 1], F32)
        stg4 = sb("stg4", [4, 128], F32)
        epsc = sb("epsc", [128, 1], F32)

        PS = [st.enter_context(nc.psum_tensor(f"PS{i}", [128, 512], F32)) for i in range(8)]

        R_MU, R_MSQ, R_VAR, R_SD, R_RSTD, R_T1 = 0, 512, 1024, 1536, 2048, 1536

        def row(off, w=KT):
            return rowb[0:1, off:off + w]

        stABCD = contextlib.ExitStack()
        stA = contextlib.ExitStack()
        stB = contextlib.ExitStack()
        stC = contextlib.ExitStack()
        stD = contextlib.ExitStack()

        q_aug = sb("q_aug", [80, NH * QT], BF16, stABCD)
        x7 = sb("x7", [128, 8 * QT], F32, stABCD)
        o_fm = sb("o_fm", [128, 8 * QT], BF16, stABCD)
        m_hist = sb("m_hist", [128, 64 * 9], F32, stABCD)
        l_hist = sb("l_hist", [128, 64 * 9], F32, stABCD)
        rmax_sc = sb("rmax_sc", [128, 64], F32, stABCD)
        sp_sc = sb("sp_sc", [128, 64], F32, stABCD)
        dsc = sb("dsc", [128, 64], F32, stABCD)
        desc = sb("desc", [128, 64], F32, stABCD)
        ltsc = sb("ltsc", [128, 64], F32, stABCD)
        scr = sb("scr", [128, 8 * KT], F32, stABCD)
        mkb = sb("mkb", [128, 4 * KT], F32, stABCD)

        wq_b = sb("wq_b", [128, 8 * D], BF16, stA)
        wk_b = sb("wk_b", [128, 8 * D], BF16, stA)
        wv_b = sb("wv_b", [128, 8 * D], BF16, stA)
        vb_b = sb("vb_b", [1, D], BF16, stA)
        ind8 = sb("ind8", [8, T], BF16, stA)
        xbuf = sb("xbuf", [128, 8 * KT], F32, stA)
        xscr = sb("xscr", [128, KT], F32, stA)
        xbb = sb("xbb", [128, 8 * KT], BF16, stA)
        xsq = sb("xsq", [128, KT], BF16, stA)
        xh = sb("xh", [128, 8 * KT], BF16, stA)
        kspill = sb("kspill", [128, 8 * KT], BF16, stA)
        vspill = sb("vspill", [128, 4 * D], BF16, stA)
        qstage = sb("qstage", [128, 8 * KT], BF16, stA)
        mu_b = sb("mu_b", [1, KT], BF16, stA)
        rs_b = sb("rs_b", [1, KT], BF16, stA)

        ld_small = S.add("sync", lambda e: e.dma_start(out=qkb_s[:], in_=qkb), dma=True)
        ld_pjb = S.add("sync", lambda e: e.dma_start(out=pjb_s[:], in_=pjb), dma=True)
        ld_cw = S.add("sync", lambda e: e.dma_start(out=cw_s[:], in_=cw), dma=True)
        ld_cb = S.add("sync", lambda e: e.dma_start(out=cb_s[:], in_=cbi), dma=True)
        ld_vb = S.add("sync", lambda e: e.dma_start(out=row(R_T1, D), in_=vbrow), dma=True)
        ld_w = {}
        for nm, src, dst in (("wq", wqTh, wq_b), ("wk", wkTh, wk_b), ("wv", wvTh, wv_b)):
            ap = dram_ap(src, 0, [[D, 128], [128 * D, 8], [1, D]])
            ld_w[nm] = S.add("gpsimd",
                             lambda e, a=ap, d=dst: e.dma_start(
                                 out=d[:].rearrange("p (c f) -> p c f", c=8), in_=a),
                             dma=True)

        c_on_b = S.add("vector", lambda e: e.memset(onescol_b[:], 1.0))
        c_o1b = S.add("vector", lambda e: e.memset(ones1_b[:], 1.0))
        c_o1f = S.add("vector", lambda e: e.memset(ones1_f[:], 1.0))
        c_onf = S.add("vector", lambda e: e.memset(onescol_f[:], 1.0))
        c_m0 = S.add("vector", lambda e: e.memset(m_hist[:, 0:64], -BIG))
        c_l0 = S.add("vector", lambda e: e.memset(l_hist[:, 0:64], 0.0))
        c_eps = S.add("vector", lambda e: e.memset(epsc[:], EPS_LN))

        g0 = S.add("gpsimd", lambda e: e.memset(identf[:], 1.0), deps=[ld_w["wv"]])
        g1 = S.add("gpsimd", lambda e: e.affine_select(
            identf[:], identf[:], [[-1, 128]], ALU.is_equal, 0.0,
            base=0, channel_multiplier=1), deps=[g0])
        identb_op = S.add("vector", lambda e: e.tensor_copy(identb[:], identf[:]),
                          deps=[g1])
        mk_ops = []
        for qc in range(4):
            a = S.add("gpsimd", lambda e, q=qc: e.memset(mkb[:, q * KT:(q + 1) * KT], BIG),
                      deps=[g1])
            b = S.add("gpsimd", lambda e, q=qc: e.affine_select(
                mkb[:, q * KT:(q + 1) * KT], mkb[:, q * KT:(q + 1) * KT],
                [[-1, KT]], ALU.is_ge, -BIG, base=128 * q, channel_multiplier=1),
                deps=[a])
            mk_ops.append(b)
        ia = S.add("gpsimd", lambda e: e.memset(ind8[:], 1.0))
        ib = S.add("gpsimd", lambda e: e.affine_select(
            ind8[:], ind8[:], [[1, T]], ALU.is_ge, 0.0,
            base=0, channel_multiplier=-KT), deps=[ia])
        ic = S.add("gpsimd", lambda e: e.affine_select(
            ind8[:], ind8[:], [[-1, T]], ALU.is_ge, 0.0,
            base=KT - 1, channel_multiplier=KT), deps=[ib])
        ind_dma = []
        for h in range(NH):
            for half in range(2):
                r0 = h * 80 + 64 + half * 8
                ind_dma.append(S.add("sync", lambda e, r=r0: e.dma_start(
                    out=k_hbm[r:r + 8, :], in_=ind8[:]), deps=[ic], dma=True))

        e0 = S.add("vector", lambda e: e.memset(Ca1[:], 2.0 ** -30), tiny=True)
        e1 = S.add("vector", lambda e: e.tensor_scalar(Ca1[:], Ca1[:], 1.0, None, ALU.add),
                   deps=[e0], tiny=True)
        e2 = S.add("vector", lambda e: e.tensor_scalar(
            Ca1[:], Ca1[:], -1.0, 2.0 ** 30, ALU.add, ALU.mult), deps=[e1], tiny=True)
        e3 = S.add("vector", lambda e: e.tensor_scalar(
            Ca1[:], Ca1[:], C52 - C23, C23, ALU.mult, ALU.add), deps=[e2], tiny=True)
        e4 = S.add("tensor", lambda e: e.matmul(
            PS[6][:, 0:1], ones1_f[:], Ca1[:], start=True, stop=True),
            deps=[e3, c_o1f])
        Ca_ready = S.add("scalar", lambda e: e.activation(Ca[:], PS[6][:, 0:1], AF.Identity),
                         deps=[e4], tiny=True)
        qb8_op = S.add("vector", lambda e: e.tensor_scalar(
            qb8[:], qkb_s[:, 0:8], 0.125, None, ALU.mult), deps=[ld_small], tiny=True)
        vb_op = S.add("vector", lambda e: e.tensor_copy(vb_b[:], row(R_T1, D)),
                      deps=[ld_vb])

        prev = {
            "xbuf": None, "xbb": None, "xsq": None, "xh": [],
            "ksp": [None] * (2 * NCH), "vsp": [None] * 4,
            "rMU": [], "rMSQ": [], "rT1": [], "rVAR": [], "rSD": [], "rRSTD": [],
            "mu_b": [], "rs_b": [], "ps45": [], "ps67": [],
        }
        kspill_dma, vspill_dma, qstage_dma = [], [], []
        x7_cp = None
        for t in range(NKT):
            xap = dram_ap(xTh, t * KT, [[T, 128], [128 * T, 8], [1, KT]])
            ldx = S.add("sync", lambda e, a=xap: e.dma_start(
                out=xbuf[:].rearrange("p (c f) -> p c f", c=8), in_=a),
                        deps=[prev["xbuf"]], dma=True)
            ldxb = S.add("gpsimd", lambda e, a=xap: e.dma_start(
                out=xbb[:].rearrange("p (c f) -> p c f", c=8), in_=a),
                         deps=[prev["xbb"]], dma=True)
            st_s = []
            for dc in range(NCH):
                st_s.append(S.add("tensor", lambda e, c=dc: e.matmul(
                    PS[4][0:1, :], onescol_b[:], xbb[:, c * KT:(c + 1) * KT],
                    start=(c == 0), stop=(c == NCH - 1)),
                    deps=[ldxb, c_on_b] + prev["ps45"]))
            st_q = []
            prev_sq = prev["xsq"]
            for dc in range(NCH):
                sq = S.add("scalar", lambda e, c=dc: e.activation(
                    xsq[:], xbuf[:, c * KT:(c + 1) * KT], AF.Square),
                    deps=[ldx, prev_sq])
                st_q.append(S.add("tensor", lambda e, c=dc: e.matmul(
                    PS[5][0:1, :], onescol_b[:], xsq[:],
                    start=(c == 0), stop=(c == NCH - 1)),
                    deps=[sq] + prev["ps45"]))
                prev_sq = st_q[-1]
            r_mu = S.add("scalar", lambda e: e.activation(
                row(R_MU), PS[4][0:1, :], AF.Identity, scale=1.0 / D),
                deps=[st_s[-1]] + prev["rMU"])
            r_ms = S.add("scalar", lambda e: e.activation(
                row(R_MSQ), PS[5][0:1, :], AF.Identity, scale=1.0 / D),
                deps=[st_q[-1]] + prev["rMSQ"])
            r_v1 = S.add("vector", lambda e: e.tensor_mul(
                row(R_T1), row(R_MU), row(R_MU)),
                deps=[r_mu, vb_op] + prev["rT1"] + prev["rSD"])
            r_v2 = S.add("vector", lambda e: e.tensor_sub(
                row(R_VAR), row(R_MSQ), row(R_T1)), deps=[r_ms, r_v1] + prev["rVAR"])
            r_sd = S.add("scalar", lambda e: e.activation(
                row(R_SD), row(R_VAR), AF.Sqrt, bias=epsc[0:1, 0:1]),
                deps=[r_v2, c_eps])
            r_rs = S.add("vector", lambda e: e.reciprocal(row(R_RSTD), row(R_SD)),
                         deps=[r_sd] + prev["rRSTD"])
            r_cb1 = S.add("vector", lambda e: e.tensor_copy(mu_b[:], row(R_MU)),
                          deps=[r_mu] + prev["mu_b"])
            r_cb2 = S.add("vector", lambda e: e.tensor_copy(rs_b[:], row(R_RSTD)),
                          deps=[r_rs] + prev["rs_b"])
            bc_mu, bc_rs = [], []
            for i in range(4):
                bc_mu.append(S.add("tensor", lambda e, j=i: e.matmul(
                    PS[6][:, j * 128:(j + 1) * 128], ones1_b[:],
                    mu_b[:, j * 128:(j + 1) * 128], start=True, stop=True),
                    deps=[r_cb1, c_o1b, Ca_ready] + prev["ps67"]))
                bc_rs.append(S.add("tensor", lambda e, j=i: e.matmul(
                    PS[7][:, j * 128:(j + 1) * 128], ones1_b[:],
                    rs_b[:, j * 128:(j + 1) * 128], start=True, stop=True),
                    deps=[r_cb2] + prev["ps67"]))
            xh_ops = []
            prev_sub = None
            for dc in range(NCH):
                aa = S.add("vector", lambda e, c=dc: e.tensor_sub(
                    xscr[:], xbuf[:, c * KT:(c + 1) * KT], PS[6][:]),
                    deps=bc_mu + [ldx, prev_sub] + (prev["xh"] if dc == 0 else []))
                bb_ = S.add("vector", lambda e, c=dc: e.tensor_mul(
                    xh[:, c * KT:(c + 1) * KT], xscr[:], PS[7][:]),
                    deps=bc_rs + [aa])
                prev_sub = bb_
                xh_ops.append(bb_)
            prev_ev = [None, None]
            kevac = []
            for kc in range(NCH):
                slot = kc % 2
                mms = []
                for dc in range(NCH):
                    mms.append(S.add("tensor", lambda e, c=dc, k=kc, s=slot: e.matmul(
                        PS[s][:], wk_b[:, c * D + k * 128: c * D + (k + 1) * 128],
                        xh[:, c * KT:(c + 1) * KT],
                        start=(c == 0), stop=(c == NCH - 1)),
                        deps=[xh_ops[dc], ld_w["wk"], prev_ev[slot]]))
                ev = S.add("scalar", lambda e, k=kc, s=slot: e.activation(
                    kspill[:, k * KT:(k + 1) * KT], PS[s][:], AF.Identity,
                    bias=qkb_s[:, 8 + k:9 + k]),
                    deps=[mms[-1], ld_small, prev["ksp"][2 * kc],
                          prev["ksp"][2 * kc + 1]])
                prev_ev[slot] = ev
                kevac.append(ev)
            ksp = []
            for kc in range(NCH):
                for hh in range(2):
                    dstap = dram_ap(k_hbmh, ((2 * kc + hh) * 80) * T + t * KT,
                                    [[T, 64], [1, KT]])
                    srcap = kspill[hh * 64:(hh + 1) * 64, kc * KT:(kc + 1) * KT]
                    ksp.append(S.add("sync", lambda e, dd=dstap, ss=srcap: e.dma_start(
                        out=dd, in_=ss), deps=[kevac[kc]], dma=True))
            kspill_dma = ksp
            prev_vev = [None, None]
            vevac = {}
            for tc in range(4):
                for nh2 in range(2):
                    slot = 2 + (2 * tc + nh2) % 2
                    mms = []
                    for dc in range(NCH):
                        mms.append(S.add("tensor",
                                         lambda e, c=dc, tc_=tc, n=nh2, s=slot: e.matmul(
                            PS[s][:],
                            xh[:, c * KT + tc_ * 128: c * KT + (tc_ + 1) * 128],
                            wv_b[:, c * D + n * 512:c * D + (n + 1) * 512],
                            start=(c == 0), stop=False),
                            deps=[xh_ops[dc], ld_w["wv"], prev_vev[slot - 2]]))
                    mb = S.add("tensor", lambda e, n=nh2, s=slot: e.matmul(
                        PS[s][:], ones1_b[:], vb_b[0:1, n * 512:(n + 1) * 512],
                        start=False, stop=True), deps=[vb_op, mms[-1]])
                    ev = S.add("scalar", lambda e, tc_=tc, n=nh2, s=slot: e.activation(
                        vspill[:, tc_ * D + n * 512: tc_ * D + (n + 1) * 512],
                        PS[s][:], AF.Identity), deps=[mb, prev["vsp"][tc]])
                    prev_vev[slot - 2] = ev
                    vevac[(tc, nh2)] = ev
            vsp = []
            for tc in range(4):
                dst = v_hbm[t * KT + tc * 128: t * KT + (tc + 1) * 128, :]
                vsp.append(S.add("gpsimd", lambda e, dd=dst, tc_=tc: e.dma_start(
                    out=dd, in_=vspill[:, tc_ * D:(tc_ + 1) * D]),
                    deps=[vevac[(tc, 0)], vevac[(tc, 1)]], dma=True))
            vspill_dma = vsp
            q_last = []
            if t == 7:
                prev_q = [None, None]
                qevac = []
                for qc in range(NCH):
                    slot = qc % 2
                    mms = []
                    for dc in range(NCH):
                        mms.append(S.add("tensor", lambda e, c=dc, k=qc, s=slot: e.matmul(
                            PS[s][:], wq_b[:, c * D + k * 128: c * D + (k + 1) * 128],
                            xh[:, c * KT:(c + 1) * KT],
                            start=(c == 0), stop=(c == NCH - 1)),
                            deps=[xh_ops[dc], ld_w["wq"],
                                  prev_q[slot] if prev_q[slot] else kevac[slot + 6]]))
                    ev = S.add("scalar", lambda e, k=qc, s=slot: e.activation(
                        qstage[:, k * KT:(k + 1) * KT], PS[s][:], AF.Identity,
                        scale=SCALE, bias=qb8[:, k:k + 1]), deps=[mms[-1], qb8_op])
                    prev_q[slot] = ev
                    qevac.append(ev)
                    q_last = mms
                for qc in range(NCH):
                    for hh in range(2):
                        h = 2 * qc + hh
                        src = qstage[hh * 64:(hh + 1) * 64, qc * KT:(qc + 1) * KT]
                        qstage_dma.append(S.add("gpsimd",
                                               lambda e, ss=src, h_=h: e.dma_start(
                            out=q_aug[0:64, h_ * QT:(h_ + 1) * QT], in_=ss),
                            deps=[qevac[qc]], dma=True))
                x7_cp = S.add("gpsimd", lambda e: e.dma_start(
                    out=x7[:], in_=xbuf[:]), deps=[ldx, xh_ops[-1]], dma=True)
            prev["xbuf"] = x7_cp if t == 7 else xh_ops[-1]
            prev["xbb"] = st_s[-1]
            prev["xsq"] = st_q[-1]
            prev["xh"] = ([q_last[-1]] if q_last else []) + [mms[-1]]
            prev["ksp"] = ksp
            prev["vsp"] = vsp
            prev["rMU"] = [r_v1, r_cb1]
            prev["rMSQ"] = [r_v2]
            prev["rT1"] = [r_v2]
            prev["rVAR"] = [r_sd]
            prev["rSD"] = [r_rs]
            prev["rRSTD"] = [r_cb2]
            prev["mu_b"] = bc_mu
            prev["rs_b"] = bc_rs
            prev["ps45"] = [r_mu, r_ms]
            prev["ps67"] = xh_ops
        xh_last = prev["xh"]

        stA.close()
        qdup = sb("qdup", [128, NH * QT], BF16, stB)
        kdup = sb("kdup", [128, NH * KT], BF16, stB)
        psink = sb("psink", [128, 8 * KT], BF16, stB)
        qd = []
        for half in range(2):
            qd.append(S.add("gpsimd", lambda e, hh=half: e.dma_start(
                out=qdup[hh * 64:(hh + 1) * 64, :], in_=q_aug[0:64, :]),
                deps=qstage_dma, dma=True))
        S.phase_gate()
        sfx = sb("sfx", [128, 128], F32, stB)
        rl5 = sb("rl5", [128, 512], F32, stB)
        rr5 = sb("rr5", [128, 512], F32, stB)
        gg5 = sb("gg5", [128, 512], F32, stB)
        wh5 = sb("wh5", [128, 512], F32, stB)
        bb5 = sb("bb5", [128, 512], F32, stB)
        bb5b = sb("bb5b", [128, 512], F32, stB)
        b1c = sb("b1c", [128, 512], BF16, stB)
        b2c = sb("b2c", [128, 512], BF16, stB)
        b2f = sb("b2f", [128, 512], F32, stB)
        lnlt = sb("lnlt", [128, 512], F32, stB)
        bstg = sb("bstg", [32, 256], BF16, stB)
        bh1 = sb("bh1", [128, 512], F32, stB)
        bh2 = sb("bh2", [128, 512], F32, stB)
        mk1 = mkb

        s_free = [None] * 8
        psink_prev = [None] * 8
        prev_kd = None
        s5_prev = [None] * 8
        s4_prev = [None] * 8
        s5_last = None
        for j in range(NKT):
            lds = []
            for half in range(2):
                ap = dram_ap(k_hbmh, j * KT, [[T, 64], [80 * T, NH], [1, KT]])
                lds.append(S.add("sync", lambda e, a=ap, hh=half: e.dma_start(
                    out=kdup[hh * 64:(hh + 1) * 64, :].rearrange(
                        "p (c f) -> p c f", c=NH), in_=a),
                    deps=[prev_kd] if prev_kd else list(kspill_dma) + qd,
                    dma=True))
            last_mm = None
            for a in range(8):
                tile8 = []
                for r2 in range(2):
                    for sub in range(2):
                        qc = 2 * r2 + sub
                        for hh in range(2):
                            h = 2 * a + hh
                            hq = h * 4 + qc
                            slot = 4 * r2 + 2 * sub + hh
                            mm = S.add("tensor",
                                       lambda e, h_=h, q_=qc, s=slot, hh_=hh: e.matmul(
                                PS[s][:],
                                qdup[hh_ * 64:(hh_ + 1) * 64,
                                     h_ * QT + q_ * 128: h_ * QT + (q_ + 1) * 128],
                                kdup[hh_ * 64:(hh_ + 1) * 64, h_ * KT:(h_ + 1) * KT],
                                start=True, stop=True, tile_position=(64 * hh_, 0)),
                                deps=[lds[hh], qd[hh], s_free[slot]])
                            last_mm = mm
                            if j == 7:
                                msrc = S.add("vector",
                                             lambda e, s=slot, q_=qc: e.tensor_tensor(
                                    scr[:, s * KT:(s + 1) * KT], PS[s][:],
                                    mk1[:, q_ * KT:(q_ + 1) * KT], ALU.min),
                                    deps=[mm] + mk_ops)
                            else:
                                msrc = mm
                            rd = S.add("vector", lambda e, s=slot, c=hq, jj=j: e.reduce_max(
                                rmax_sc[:, c:c + 1],
                                (scr[:, s * KT:(s + 1) * KT] if jj == 7 else PS[s][:]),
                                axis=AX), deps=[msrc], tiny=True)
                            tile8.append((hq, slot, msrc, rd))
                c0 = 8 * a
                s1 = S.add("vector", lambda e, c=c0, jj=j: e.tensor_max(
                    m_hist[:, (jj + 1) * 64 + c:(jj + 1) * 64 + c + 8],
                    m_hist[:, jj * 64 + c:jj * 64 + c + 8],
                    rmax_sc[:, c:c + 8]),
                    deps=[x[3] for x in tile8] + ([c_m0] if j == 0 else []), tiny=True)
                s1n = S.add("vector", lambda e, c=c0, jj=j: e.tensor_scalar(
                    dsc[:, c:c + 8],
                    m_hist[:, (jj + 1) * 64 + c:(jj + 1) * 64 + c + 8],
                    -1.0, None, ALU.mult), deps=[s1], tiny=True)
                exps = []
                for (hq, slot, msrc, rd) in tile8:
                    ex = S.add("scalar", lambda e, s=slot, c=hq, jj=j: e.activation(
                        psink[:, s * KT:(s + 1) * KT],
                        (scr[:, s * KT:(s + 1) * KT] if jj == 7 else PS[s][:]),
                        AF.Exp, bias=dsc[:, c:c + 1], accum_out=sp_sc[:, c:c + 1]),
                        deps=[s1n, msrc, s5_prev[a], psink_prev[slot]])
                    exps.append(ex)
                    s_free[slot] = ex
                    psink_prev[slot] = ex
                s2 = S.add("vector", lambda e, c=c0, jj=j: e.tensor_sub(
                    dsc[:, c:c + 8], m_hist[:, jj * 64 + c:jj * 64 + c + 8],
                    m_hist[:, (jj + 1) * 64 + c:(jj + 1) * 64 + c + 8]),
                    deps=[s1, exps[-1]], tiny=True)
                s3 = S.add("scalar", lambda e, c=c0: e.activation(
                    desc[:, c:c + 8], dsc[:, c:c + 8], AF.Exp),
                    deps=[s2, s4_prev[a]], tiny=True)
                s4 = S.add("vector", lambda e, c=c0, jj=j: e.tensor_mul(
                    ltsc[:, c:c + 8], l_hist[:, jj * 64 + c:jj * 64 + c + 8],
                    desc[:, c:c + 8]),
                    deps=[s3] + ([c_l0] if j == 0 else []), tiny=True)
                s5 = S.add("vector", lambda e, c=c0, jj=j: e.tensor_add(
                    l_hist[:, (jj + 1) * 64 + c:(jj + 1) * 64 + c + 8],
                    ltsc[:, c:c + 8], sp_sc[:, c:c + 8]),
                    deps=[s4] + exps, tiny=True)
                s5_prev[a] = s5
                s4_prev[a] = s4
                s5_last = s5
            prev_kd = last_mm

        pm1 = S.add("vector", lambda e: e.reciprocal(rl5[:], l_hist[:, 64:576]),
                    deps=[s5_last])
        pm2 = S.add("vector", lambda e: e.tensor_mul(rr5[:], l_hist[:, 0:512], rl5[:]),
                    deps=[pm1])
        pm3 = S.add("vector", lambda e: e.tensor_scalar(
            gg5[:], rr5[:], -1.0, 1.0, ALU.mult, ALU.add), deps=[pm2])
        sfx_prev = S.add("vector", lambda e: e.memset(sfx[:, 0:64], 1.0), deps=[pm3])
        wops = []
        for jj in range(7, -1, -1):
            cur = jj % 2
            w_ = S.add("vector", lambda e, j_=jj, c=(jj + 1) % 2: e.tensor_mul(
                wh5[:, j_ * 64:(j_ + 1) * 64], rr5[:, j_ * 64:(j_ + 1) * 64],
                sfx[:, c * 64:(c + 1) * 64]), deps=[sfx_prev, pm2])
            if jj > 0:
                sfx_prev = S.add("vector", lambda e, j_=jj, c=(jj + 1) % 2, n=jj % 2:
                                 e.tensor_mul(
                    sfx[:, n * 64:(n + 1) * 64], sfx[:, c * 64:(c + 1) * 64],
                    gg5[:, j_ * 64:(j_ + 1) * 64]), deps=[w_, pm3])
            wops.append(w_)
        pw1 = S.add("vector", lambda e: e.tensor_scalar(
            wh5[:], wh5[:], 1e-38, None, ALU.max), deps=wops)
        pw2 = S.add("scalar", lambda e: e.activation(bb5[:], wh5[:], AF.Ln), deps=[pw1])
        pl1 = S.add("scalar", lambda e: e.activation(
            lnlt[:, 0:64], l_hist[:, 512:576], AF.Ln), deps=[pm1])
        tile_l = [pl1]
        for jj in range(1, 8):
            tile_l.append(S.add("vector", lambda e, j_=jj: e.tensor_copy(
                lnlt[:, j_ * 64:(j_ + 1) * 64], lnlt[:, 0:64]), deps=[pl1]))
        pb1 = S.add("vector", lambda e: e.tensor_sub(bb5b[:], bb5[:], lnlt[:]),
                    deps=[pw2] + tile_l)
        pb2 = S.add("vector", lambda e: e.tensor_sub(bb5[:], bb5b[:], m_hist[:, 64:576]),
                    deps=[pb1])
        pb3 = S.add("vector", lambda e: e.tensor_scalar(
            bb5b[:], bb5[:], -80.0, 80.0, ALU.max, ALU.min), deps=[pb2])
        pb4 = S.add("vector", lambda e: e.tensor_copy(b1c[:], bb5b[:]), deps=[pb3])
        pb5 = S.add("vector", lambda e: e.tensor_sub(b2f[:], bb5b[:], b1c[:]), deps=[pb4])
        pb6 = S.add("vector", lambda e: e.tensor_copy(b2c[:], b2f[:]), deps=[pb5])
        bstage_dma = []
        prev_tr = None
        reord = []
        for h in range(NH):
            for qc in range(4):
                hq = h * 4 + qc
                reord.append(S.add("vector", lambda e, q_=hq: e.tensor_copy(
                    bh1[:, q_ * 8:(q_ + 1) * 8],
                    bass.AP(bb5b, q_, [[512, 128], [64, 8]])), deps=[pb3]))
                reord.append(S.add("vector", lambda e, q_=hq: e.tensor_copy(
                    bh2[:, q_ * 8:(q_ + 1) * 8],
                    bass.AP(b2f, q_, [[512, 128], [64, 8]])), deps=[pb5]))
        prev_sdma = [None, None]
        for h in range(NH):
            for qc in range(4):
                hq = h * 4 + qc
                tr1 = S.add("tensor", lambda e, q_=hq: e.transpose(
                    PS[4][0:8, 0:128], bh1[:, q_ * 8:(q_ + 1) * 8], identf[:]),
                    deps=[reord[2 * hq], identb_op, prev_tr, s_free[4]])
                ev1 = S.add("scalar", lambda e: e.activation(
                    bstg[0:8, 0:128], PS[4][0:8, 0:128], AF.Identity),
                    deps=[tr1, prev_sdma[0]])
                tr2 = S.add("tensor", lambda e, q_=hq: e.transpose(
                    PS[5][0:8, 0:128], bh2[:, q_ * 8:(q_ + 1) * 8], identf[:]),
                    deps=[reord[2 * hq + 1], ev1, s_free[5]])
                ev2 = S.add("scalar", lambda e: e.activation(
                    bstg[0:8, 128:256], PS[5][0:8, 0:128], AF.Identity),
                    deps=[tr2, prev_sdma[1]])
                prev_tr = ev2
                for which, ev in ((0, ev1), (1, ev2)):
                    dstap = q_aug[64 + which * 8:72 + which * 8,
                                  h * QT + qc * 128: h * QT + (qc + 1) * 128]
                    srcap = bstg[0:8, which * 128:(which + 1) * 128]
                    dmop = S.add("gpsimd", lambda e, dd=dstap, ss=srcap: e.dma_start(
                        out=dd, in_=ss), deps=[ev], dma=True)
                    bstage_dma.append(dmop)
                    prev_sdma[which] = dmop

        if debug:
            S.add("sync", lambda e: e.dma_start(
                out=dbg["dbg_lhist"], in_=l_hist[:, 0:576]), deps=[pm2], dma=True)
            S.add("sync", lambda e: e.dma_start(
                out=dbg["dbg_mhist"], in_=m_hist[:, 0:576]), deps=[pb3], dma=True)
            S.add("sync", lambda e: e.dma_start(
                out=dbg["dbg_b"], in_=q_aug[64:80, :]),
                deps=list(bstage_dma), dma=True)
        S.phase_gate()
        stB.close()
        kaug = sb("kaug", [80, NH * KT], BF16, stC)
        o_acc = [sb("o_accA", [128, 8 * KT], F32, stC),
                 sb("o_accB", [128, 8 * KT], F32, stC)]
        vtile = sb("vtile", [128, 4 * D], BF16, stC)
        ppb = sb("ppb", [128, 4 * KT], BF16, stC)
        mk2 = mkb
        mk2_ops = []
        for c_ in range(4):
            a_ = S.add("gpsimd", lambda e, q=c_: e.memset(
                mkb[:, q * KT:(q + 1) * KT], BIG), deps=[s5_last] + list(bstage_dma))
            b_ = S.add("gpsimd", lambda e, q=c_: e.affine_select(
                mkb[:, q * KT:(q + 1) * KT], mkb[:, q * KT:(q + 1) * KT],
                [[1, KT]], ALU.is_ge, -BIG, base=-128 * q, channel_multiplier=-1),
                deps=[a_])
            mk2_ops.append(b_)
        prev_ka = None
        prev_vt = None
        o_cast = {}
        pp_free = None
        pv_bank_free = [None] * 4
        oacc_prev = {}
        for grp in range(2):
            for j in range(NKT):
                kap = dram_ap(k_hbmh, j * KT, [[T, 80], [80 * T, NH], [1, KT]])
                lk = S.add("sync", lambda e, a=kap: e.dma_start(
                    out=kaug[:].rearrange("p (c f) -> p c f", c=NH), in_=a),
                    deps=([prev_ka] if prev_ka else list(bstage_dma) + ind_dma),
                    dma=True)
                vap = dram_ap(v_hbmh, j * KT * D, [[D, 128], [128 * D, 4], [1, D]])
                lv = S.add("sync", lambda e, a=vap: e.dma_start(
                    out=vtile[:].rearrange("p (c f) -> p c f", c=4), in_=a),
                    deps=([prev_vt] if prev_vt else list(vspill_dma)),
                    dma=True)
                for hh in range(8):
                    h = grp * 8 + hh
                    pair = hh // 2
                    chunk = grp * 4 + pair
                    slot = 4 + pair
                    smm = []
                    for c in range(4):
                        smm.append(S.add("tensor", lambda e, h_=h, c_=c: e.matmul(
                            PS[c_][:],
                            kaug[:, h_ * KT + c_ * 128: h_ * KT + (c_ + 1) * 128],
                            q_aug[:, h_ * QT:(h_ + 1) * QT],
                            start=True, stop=True),
                            deps=[lk, s_free[c_]] + list(bstage_dma)))
                    if j == 7:
                        pre = []
                        for c in range(4):
                            pre.append(S.add("vector", lambda e, c_=c: e.tensor_tensor(
                                scr[:, c_ * KT:(c_ + 1) * KT], PS[c_][:],
                                mk2[:, c_ * KT:(c_ + 1) * KT], ALU.min),
                                deps=[smm[c]] + mk2_ops))
                        exd = pre
                    else:
                        exd = smm
                    exps = []
                    for c in range(4):
                        ex = S.add("scalar", lambda e, c_=c, jj=j: e.activation(
                            ppb[:, c_ * KT:(c_ + 1) * KT],
                            (scr[:, c_ * KT:(c_ + 1) * KT] if jj == 7 else PS[c_][:]),
                            AF.Exp),
                            deps=[exd[c]] + ([pp_free] if (pp_free and c == 0) else []))
                        exps.append(ex)
                        s_free[c] = ex
                    pvs = []
                    for c in range(4):
                        pvs.append(S.add("tensor",
                                         lambda e, h_=h, c_=c, s=slot: e.matmul(
                            PS[s][64 * (h_ % 2):64 * (h_ % 2) + 64, :],
                            vtile[:, c_ * D + h_ * HD: c_ * D + (h_ + 1) * HD],
                            ppb[:, c_ * KT:(c_ + 1) * KT],
                            start=(c_ == 0), stop=(c_ == 3),
                            tile_position=(0, 64 * (h_ % 2))),
                            deps=[exps[c_], lv, pv_bank_free[pair]]))
                    pp_free = pvs[-1]
                    prev_ka = smm[-1]
                    prev_vt = pvs[-1]
                    if h % 2 == 1:
                        src_b = j % 2
                        dst_b = (j + 1) % 2
                        if j == 0:
                            acc = S.add("vector", lambda e, ch=chunk, s=slot: e.tensor_copy(
                                o_acc[1][:, ch % 4 * KT:(ch % 4 + 1) * KT], PS[s][:]),
                                deps=pvs + [oacc_prev.get((1, chunk))])
                        else:
                            acc = S.add("vector", lambda e, ch=chunk, s=slot,
                                        sb_=src_b, db_=dst_b: e.tensor_add(
                                o_acc[db_][:, ch % 4 * KT:(ch % 4 + 1) * KT],
                                o_acc[sb_][:, ch % 4 * KT:(ch % 4 + 1) * KT],
                                PS[s][:]),
                                deps=pvs + [oacc_prev.get((dst_b, chunk))])
                        oacc_prev[((j + 1) % 2, chunk)] = acc
                        pv_bank_free[pair] = acc
                        if j == NKT - 1:
                            o_cast[chunk] = S.add("vector",
                                                  lambda e, ch=chunk: e.tensor_copy(
                                o_fm[:, ch * KT:(ch + 1) * KT],
                                o_acc[0][:, ch % 4 * KT:(ch % 4 + 1) * KT]),
                                deps=[acc])

        S.phase_gate()
        stC.close()
        pj_b = sb("pj_b", [128, 8 * D], BF16, stD)
        xat_sb = sb("xat_sb", [128, 8 * QT], F32, stD)
        ld_pj = S.add("gpsimd", lambda e: e.dma_start(
            out=pj_b[:].rearrange("p (c f) -> p c f", c=8),
            in_=dram_ap(projTh, 0, [[D, 128], [128 * D, 8], [1, D]])),
            dma=True)
        prev_pd = [None, None]
        xat_ops = []
        for oc in range(NCH):
            slot = oc % 2
            mms = []
            for dc in range(NCH):
                mms.append(S.add("tensor", lambda e, c=dc, o=oc, s=slot: e.matmul(
                    PS[s][:], pj_b[:, c * D + o * 128: c * D + (o + 1) * 128],
                    o_fm[:, c * QT:(c + 1) * QT],
                    start=(c == 0), stop=(c == NCH - 1)),
                    deps=[ld_pj, o_cast[c], prev_pd[slot],
                          s_free[slot] if oc < 2 else None]))
            ev = S.add("vector", lambda e, o=oc, s=slot: e.tensor_add(
                xat_sb[:, o * QT:(o + 1) * QT], PS[s][:], x7[:, o * QT:(o + 1) * QT]),
                deps=[mms[-1], x7_cp])
            prev_pd[slot] = ev
            xat_ops.append(ev)
        xat_dma = S.add("sync", lambda e: e.dma_start(
            out=dram_ap(xattnh, 0, [[QT, 128], [128 * QT, 8], [1, QT]]),
            in_=xat_sb[:].rearrange("p (c f) -> p c f", c=8)),
            deps=xat_ops, dma=True)

        if debug:
            S.add("sync", lambda e: e.dma_start(out=dbg["dbg_ofm"], in_=o_fm[:]),
                  deps=list(o_cast.values()), dma=True)
            S.add("sync", lambda e: e.dma_start(out=dbg["dbg_x1"], in_=xat_sb[:]),
                  deps=xat_ops, dma=True)
            S.add("sync", lambda e: e.dma_start(out=dbg["dbg_xh"], in_=xh[:]),
                  deps=xh_last, dma=True)

        joins = S.phase_gate()
        stD.close()
        stABCD.close()

        stF = contextlib.ExitStack()

        def sbF(name, shape, dt=F32):
            return sb(name, shape, dt, stF)

        w1q = sbF("w1q", [128, 8 * HID], BF16)
        w2q = sbF("w2q", [128, 32 * D], BF16)
        asum = sbF("asum", [128, 40], F32)
        stW = contextlib.ExitStack()
        wstr = sb("wstr", [128, HID], F32, stW)

        RF_MU, RF_MSQ, RF_VAR, RF_RSTD, RF_MX, RF_MN = 0, 512, 1024, 1536, 1024, 2048

        def tern(whbm, nrows, wq_dst, colw, sw_col, cval, rscratch, first_deps):
            prevop = None
            absops = []
            for r in range(nrows):
                ap = dram_ap(whbm, r * 128 * colw, [[colw, 128], [1, colw]])
                ld = S.add("sync", lambda e, a=ap: e.dma_start(
                    out=wstr[:, 0:colw], in_=a),
                    deps=[prevop] if prevop else first_deps, dma=True)
                ab = S.add("scalar", lambda e, rr=r, n=colw: e.activation(
                    wstr[:, 0:n], wstr[:, 0:n], AF.Abs,
                    accum_out=asum[:, rr:rr + 1]), deps=[ld])
                prevop = ab
                absops.append(ab)
            red = S.add("vector", lambda e, n=nrows: e.tensor_reduce(
                asum[:, 39:40], asum[:, 0:n], axis=AX, op=ALU.add),
                deps=absops, tiny=True)
            mm = S.add("tensor", lambda e: e.matmul(
                PS[6][0:1, 0:1], onescol_f[:], asum[:, 39:40], start=True, stop=True),
                deps=[red, c_onf])
            sc1 = S.add("scalar", lambda e, n=nrows, cwid=colw: e.activation(
                cval[:], PS[6][0:1, 0:1], AF.Identity, scale=1.0 / (n * 128 * cwid)),
                deps=[mm], tiny=True)
            sc2 = S.add("vector", lambda e: e.tensor_scalar(
                cval[:], cval[:], 1e-5, None, ALU.max), deps=[sc1], tiny=True)
            sc3 = S.add("vector", lambda e: e.reciprocal(rscratch[:], cval[:]),
                        deps=[sc2], tiny=True)
            bc = S.add("tensor", lambda e: e.matmul(
                PS[6][:, 0:1], ones1_f[:], rscratch[:], start=True, stop=True),
                deps=[sc3])
            bce = S.add("scalar", lambda e: e.activation(
                sw_col[:], PS[6][:, 0:1], AF.Identity), deps=[bc], tiny=True)
            prev2 = bce
            outs = []
            for r in range(nrows):
                ap = dram_ap(whbm, r * 128 * colw, [[colw, 128], [1, colw]])
                ld = S.add("sync", lambda e, a=ap: e.dma_start(
                    out=wstr[:, 0:colw], in_=a), deps=[prev2], dma=True)
                t1 = S.add("vector", lambda e, n=colw: e.tensor_scalar(
                    wstr[:, 0:n], wstr[:, 0:n], sw_col[:], None, ALU.mult),
                    deps=[ld, bce])
                t2 = S.add("vector", lambda e, n=colw: e.tensor_scalar(
                    wstr[:, 0:n], wstr[:, 0:n], Ca[:], Ca[:], ALU.add, ALU.subtract),
                    deps=[t1, Ca_ready])
                t3 = S.add("vector", lambda e, rr=r, n=colw: e.tensor_scalar(
                    wq_dst[:, rr * n:(rr + 1) * n], wstr[:, 0:n],
                    1.0, -1.0, ALU.min, ALU.max), deps=[t2])
                prev2 = t3
                outs.append(t3)
            return outs, sc2, bce

        w1q_ops, c1_op, w1q_bce = tern(w1Th, NCH, w1q, HID, sw1, c1_s, rc1, joins)
        w2q_ops, c2_op, w2q_bce = tern(w2Th, 32, w2q, D, sw2, c2_s, rc2,
                                       [w1q_ops[-1]])
        S.phase_gate()
        stW.close()
        u_all = sbF("u_all", [128, 32 * TTF], F32)
        xload = sbF("xload", [128, 8 * W], F32)
        x1f = sbF("x1f", [128, 8 * W], F32)
        x1ch = sbF("x1ch", [128, W], F32)
        n1 = sbF("n1", [128, 8 * W], BF16)
        csb = sbF("csb", [128, W], BF16)
        mxb = sbF("mxb", [128, 512], F32)
        mnb = sbF("mnb", [128, 512], F32)
        bcA = sbF("bcA", [128, W], F32)
        bcB = sbF("bcB", [128, W], F32)
        bcC = sbF("bcC", [128, W], F32)
        bcS = sbF("bcS", [128, TTF], F32)
        bcI = sbF("bcI", [128, TTF], F32)
        tca = sbF("tca", [128, TTF], F32)
        tcb = sbF("tcb", [128, TTF], F32)
        n2 = sbF("n2", [128, TTF], BF16)
        osb = sbF("osb", [128, TTF], F32)

        prev_tile_out = []
        prev_consume = joins
        prev_stats = []
        prev_mx = []
        prev_rowfree = [w1q_bce, w2q_bce]
        prev_stg = [[]]
        out_dmas = []
        for ti in range(NFT):
            t0 = ti * TTF
            twf = TTF
            xl_ops = []
            if ti == 0:
                z = S.add("vector", lambda e: e.memset(xload[:], 0.0),
                          deps=prev_consume)
                xl_ops.append(z)
                lo_eff, off_eff, n_eff = 0, 1, W - 1
            elif ti == NFT - 1:
                z = S.add("vector", lambda e: e.memset(xload[:], 0.0),
                          deps=prev_consume)
                xl_ops.append(z)
                lo_eff, off_eff, n_eff = t0 - 1, 0, W - 1
            else:
                lo_eff, off_eff, n_eff = t0 - 1, 0, W
            pieces = []
            if lo_eff < ATT0:
                n_a = min(n_eff, ATT0 - lo_eff)
                pieces.append(("x", lo_eff, off_eff, n_a))
                if n_eff > n_a:
                    pieces.append(("a", lo_eff + n_a, off_eff + n_a, n_eff - n_a))
            else:
                pieces.append(("a", lo_eff, off_eff, n_eff))
            base_deps = xl_ops if xl_ops else list(prev_consume)
            for kind, src0, dsto, n in pieces:
                if kind == "x":
                    ap = dram_ap(xTh, src0, [[T, 128], [128 * T, 8], [1, n]])
                else:
                    ap = dram_ap(xattnh, src0 - ATT0,
                                 [[QT, 128], [128 * QT, 8], [1, n]])
                dstp = bass.AP(xload, dsto, [[8 * W, 128], [W, 8], [1, n]])
                slow = n < 16
                xl_ops.append(S.add("sync", lambda e, a=ap, dd=dstp, sl=slow: e.dma_start(
                    out=dd, in_=a, allow_slow_non_contiguous=sl),
                    deps=base_deps + [xat_dma], dma=True))
            x1_ops = []
            for dc in range(NCH):
                x1_ops.append(S.add("vector", lambda e, c=dc: e.tensor_scalar(
                    x1f[:, c * W:(c + 1) * W], xload[:, c * W:(c + 1) * W],
                    pjb_s[:, c:c + 1], None, ALU.add),
                    deps=xl_ops + [ld_pjb] + prev_consume))
            st_mm, sq_mm = [], []
            prev_cs = None
            prev_cq = None
            for dc in range(NCH):
                cbo = S.add("vector", lambda e, c=dc: e.tensor_copy(
                    csb[:], x1f[:, c * W:(c + 1) * W]),
                    deps=[x1_ops[dc], prev_cq] + prev_stats)
                s_mm = S.add("tensor", lambda e, c=dc: e.matmul(
                    PS[4][0:1, 0:W], onescol_b[:], csb[:],
                    start=(c == 0), stop=(c == NCH - 1)), deps=[cbo])
                sqo = S.add("scalar", lambda e, c=dc: e.activation(
                    csb[:], x1f[:, c * W:(c + 1) * W], AF.Square),
                    deps=[x1_ops[dc], s_mm] + prev_stats)
                q_mm = S.add("tensor", lambda e, c=dc: e.matmul(
                    PS[5][0:1, 0:W], onescol_b[:], csb[:],
                    start=(c == 0), stop=(c == NCH - 1)), deps=[sqo])
                st_mm.append(s_mm)
                sq_mm.append(q_mm)
                prev_cs, prev_cq = s_mm, q_mm
            r_mu = S.add("scalar", lambda e: e.activation(
                row(RF_MU, W), PS[4][0:1, 0:W], AF.Identity, scale=1.0 / D),
                deps=[st_mm[-1]] + prev_rowfree)
            r_ms = S.add("scalar", lambda e: e.activation(
                row(RF_MSQ, W), PS[5][0:1, 0:W], AF.Identity, scale=1.0 / D),
                deps=[sq_mm[-1]] + prev_rowfree)
            prev_stats = [r_mu, r_ms]
            r_v1 = S.add("vector", lambda e: e.tensor_mul(
                row(RF_VAR, W), row(RF_MU, W), row(RF_MU, W)), deps=[r_mu])
            r_v2 = S.add("vector", lambda e: e.tensor_sub(
                row(RF_VAR, W), row(RF_MSQ, W), row(RF_VAR, W)), deps=[r_ms, r_v1])
            r_sd = S.add("scalar", lambda e: e.activation(
                row(RF_RSTD, W), row(RF_VAR, W), AF.Sqrt, bias=epsc[0:1, 0:1]),
                deps=[r_v2, c_eps])
            r_rs = S.add("vector", lambda e: e.reciprocal(
                row(RF_RSTD, W), row(RF_RSTD, W)), deps=[r_sd])
            zp1 = S.add("vector", lambda e: e.memset(mxb[:, W:512], -BIG),
                        deps=prev_mx)
            zp2 = S.add("vector", lambda e: e.memset(mnb[:, W:512], BIG),
                        deps=prev_mx)
            x13 = bass.AP(x1f, 0, [[8 * W, 128], [1, W], [W, 8]])
            fmx = S.add("vector", lambda e, a=x13: e.tensor_reduce(
                mxb[:, 0:W], a, axis=AX, op=ALU.max), deps=x1_ops + [zp1])
            fmn = S.add("vector", lambda e, a=x13: e.tensor_reduce(
                mnb[:, 0:W], a, axis=AX, op=ALU.min), deps=x1_ops + [zp2])

            def cross_max(buf, dep_ops, use_min, rowoff, extra=()):
                rds = []
                for i4 in range(4):
                    tr = S.add("tensor", lambda e, i=i4, b=buf: e.transpose(
                        PS[6][:, 0:128], b[:, i * 128:(i + 1) * 128], identf[:]),
                        deps=list(dep_ops) + [identb_op] + list(extra) +
                             prev_rowfree + ([rds[-1]] if rds else []))
                    rd = S.add("vector", lambda e, i=i4, m=use_min: e.tensor_reduce(
                        asum[:, 32 + i:33 + i], PS[6][:, 0:128], axis=AX,
                        op=(ALU.min if m else ALU.max)), deps=[tr], tiny=True)
                    rds.append(rd)
                tr2 = S.add("tensor", lambda e: e.transpose(
                    PS[6][0:4, 128:256], asum[:, 32:36], identf[:]),
                    deps=rds + prev_stg[0])
                ev = S.add("scalar", lambda e: e.activation(
                    stg4[:], PS[6][0:4, 128:256], AF.Identity),
                    deps=[tr2] + prev_stg[0], tiny=True)
                dms = []
                for i4 in range(4):
                    dms.append(S.add("gpsimd", lambda e, ro=rowoff, i=i4: e.dma_start(
                        out=rowb[0:1, ro + i * 128: ro + (i + 1) * 128],
                        in_=stg4[i:i + 1, :]), deps=[ev] + prev_rowfree, dma=True))
                prev_stg[0] = list(dms)
                dm = S.barrier("gpsimd", dms)
                return dm, tr2

            mx_dma, mx_tr = cross_max(mxb, [fmx, r_sd], False, RF_MX)
            mn_dma, mn_tr = cross_max(mnb, [fmn], True, RF_MN, [mx_tr])
            prev_mx = [mn_tr]
            q1 = S.add("vector", lambda e: e.tensor_sub(
                row(RF_MX, W), row(RF_MX, W), row(RF_MU, W)), deps=[mx_dma, r_v1])
            q2 = S.add("vector", lambda e: e.tensor_sub(
                row(RF_MN, W), row(RF_MU, W), row(RF_MN, W)), deps=[mn_dma, q1])
            q3 = S.add("vector", lambda e: e.tensor_max(
                row(RF_MX, W), row(RF_MX, W), row(RF_MN, W)), deps=[q2])
            q4 = S.add("vector", lambda e: e.tensor_mul(
                row(RF_MX, W), row(RF_MX, W), row(RF_RSTD, W)), deps=[q3, r_rs])
            q5 = S.add("vector", lambda e: e.tensor_scalar(
                row(RF_MX, W), row(RF_MX, W), 1e-5, None, ALU.max), deps=[q4])
            q6 = S.add("vector", lambda e: e.reciprocal(
                row(RF_MN, W), row(RF_MX, W)), deps=[q5])
            q7 = S.add("vector", lambda e: e.tensor_scalar(
                row(RF_MN, W), row(RF_MN, W), 127.0, None, ALU.mult), deps=[q6])
            q8 = S.add("vector", lambda e: e.tensor_scalar(
                row(RF_MX, W), row(RF_MX, W), c1_s[0:1, 0:1], 1.0 / 127.0,
                ALU.mult, ALU.mult), deps=[q6, c1_op])
            q9 = S.add("vector", lambda e: e.tensor_mul(
                row(RF_MSQ, W), row(RF_RSTD, W), row(RF_MN, W)), deps=[q7, r_v2])
            q10 = S.add("vector", lambda e: e.tensor_mul(
                row(RF_MU, W), row(RF_MU, W), row(RF_MSQ, W)), deps=[q9, r_v1])
            q11 = S.add("vector", lambda e: e.tensor_scalar(
                row(RF_MU, W), row(RF_MU, W), -1.0, None, ALU.mult), deps=[q10])
            bc_ops = []
            prev_bc = None
            for ro, dstb, dep in ((RF_MSQ, bcA, q9), (RF_MU, bcB, q11), (RF_MX, bcC, q8)):
                mm1 = S.add("tensor", lambda e, r_=ro: e.matmul(
                    PS[6][:, 0:W], ones1_f[:], row(r_, W), start=True, stop=True),
                    deps=[dep, q11, prev_bc])
                ev = S.add("scalar", lambda e, dd=dstb: e.activation(
                    dd[:, 0:W], PS[6][:, 0:W], AF.Identity), deps=[mm1])
                prev_bc = ev
                bc_ops.append(ev)
            n1_ops = []
            prev_chain = None
            for dc in range(NCH):
                a1 = S.add("vector", lambda e, c=dc: e.tensor_mul(
                    x1ch[:], x1f[:, c * W:(c + 1) * W], bcA[:]),
                    deps=[bc_ops[0], x1_ops[dc], prev_chain])
                a2 = S.add("vector", lambda e: e.tensor_add(
                    x1ch[:], x1ch[:], bcB[:]), deps=[a1, bc_ops[1]])
                a3 = S.add("vector", lambda e: e.tensor_scalar(
                    x1ch[:], x1ch[:], Ca[:], Ca[:], ALU.add, ALU.subtract),
                    deps=[a2, Ca_ready])
                a4 = S.add("vector", lambda e, c=dc: e.tensor_mul(
                    n1[:, c * W:(c + 1) * W], x1ch[:], bcC[:]),
                    deps=[a3, bc_ops[2]])
                prev_chain = a4
                n1_ops.append(a4)
            if ti == 0:
                hoff, hN = 1, W - 1
            elif ti == NFT - 1:
                hoff, hN = 0, W - 1
            else:
                hoff, hN = 0, W
            conv_ops = []
            hfree = [None] * 4
            for cc in range(32):
                slot = cc % 4
                mms = []
                for dc in range(NCH):
                    mms.append(S.add("tensor",
                                     lambda e, c=dc, k=cc, s=slot, ho=hoff, n=hN: e.matmul(
                        PS[s][:, ho:ho + n],
                        w1q[:, c * HID + k * 128: c * HID + (k + 1) * 128],
                        n1[:, c * W + ho: c * W + ho + n],
                        start=(c == 0), stop=(c == NCH - 1)),
                        deps=[n1_ops[dc], w1q_ops[-1], hfree[slot]]))
                zops = []
                if ti == 0:
                    zops.append(S.add("vector", lambda e, s=slot: e.memset(
                        PS[s][:, 0:1], 0.0), deps=[mms[-1]], tiny=True))
                if ti == NFT - 1:
                    zops.append(S.add("vector", lambda e, s=slot: e.memset(
                        PS[s][:, W - 1:W], 0.0), deps=[mms[-1]], tiny=True))
                c1o = S.add("vector", lambda e, s=slot, k=cc: e.tensor_scalar(
                    tca[:], PS[s][:, 2:2 + TTF], cw_s[:, 64 + k:65 + k],
                    None, ALU.mult), deps=mms + zops + [ld_cw])
                c2o = S.add("vector", lambda e, s=slot, k=cc: e.scalar_tensor_tensor(
                    tcb[:], PS[s][:, 1:1 + TTF], cw_s[:, 32 + k:33 + k],
                    tca[:], ALU.mult, ALU.add), deps=[c1o])
                c3o = S.add("vector", lambda e, s=slot, k=cc: e.scalar_tensor_tensor(
                    u_all[:, k * TTF:(k + 1) * TTF], PS[s][:, 0:TTF],
                    cw_s[:, k:k + 1], tcb[:], ALU.mult, ALU.add),
                    deps=[c2o] + (prev_consume if cc == 0 else []))
                hfree[slot] = c3o
                si = S.add("scalar", lambda e, k=cc: e.activation(
                    u_all[:, k * TTF:(k + 1) * TTF], u_all[:, k * TTF:(k + 1) * TTF],
                    AF.Silu, bias=cb_s[:, k:k + 1]), deps=[c3o, ld_cb])
                conv_ops.append(si)
            zp = S.add("vector", lambda e: e.memset(mxb[:, 0:512], 0.0),
                       deps=[q1, q3])
            u3 = bass.AP(u_all, 0, [[32 * TTF, 128], [1, TTF], [TTF, 32]])
            f2 = S.add("vector", lambda e, a=u3: e.tensor_reduce(
                mxb[:, 0:TTF], a, axis=AX, op=ALU.max, apply_absolute_value=True),
                deps=conv_ops + [zp])
            m2_dma, m2_tr = cross_max(mxb, [f2], False, RF_MX, [q8, mn_tr])
            prev_mx = [m2_tr]
            w1r = S.add("vector", lambda e: e.tensor_scalar(
                row(RF_MX, TTF), row(RF_MX, TTF), 1e-5, None, ALU.max), deps=[m2_dma])
            w2r = S.add("vector", lambda e: e.reciprocal(
                row(RF_MN, TTF), row(RF_MX, TTF)), deps=[w1r, q9])
            w3r = S.add("vector", lambda e: e.tensor_scalar(
                row(RF_MN, TTF), row(RF_MN, TTF), 127.0, None, ALU.mult), deps=[w2r])
            w4r = S.add("vector", lambda e: e.tensor_scalar(
                row(RF_MX, TTF), row(RF_MX, TTF), c2_s[0:1, 0:1], 1.0 / 127.0,
                ALU.mult, ALU.mult), deps=[w1r, c2_op, w2r])
            bs_mm = S.add("tensor", lambda e: e.matmul(
                PS[6][:, 0:TTF], ones1_f[:], row(RF_MN, TTF), start=True, stop=True),
                deps=[w3r, bc_ops[-1]])
            bs_ev = S.add("scalar", lambda e: e.activation(
                bcS[:], PS[6][:, 0:TTF], AF.Identity), deps=[bs_mm])
            bi_mm = S.add("tensor", lambda e: e.matmul(
                PS[6][:, 0:TTF], ones1_f[:], row(RF_MX, TTF), start=True, stop=True),
                deps=[w4r, bs_ev])
            bi_ev = S.add("scalar", lambda e: e.activation(
                bcI[:], PS[6][:, 0:TTF], AF.Identity), deps=[bi_mm])
            y_mms = [[] for _ in range(8)]
            prev_q2 = None
            for cc in range(32):
                uqo = S.add("vector", lambda e, k=cc: e.tensor_mul(
                    tcb[:], u_all[:, k * TTF:(k + 1) * TTF], bcS[:]),
                    deps=[conv_ops[cc], bs_ev, prev_q2])
                n2o = S.add("vector", lambda e: e.tensor_scalar(
                    n2[:], tcb[:], Ca[:], Ca[:], ALU.add, ALU.subtract),
                    deps=[uqo, Ca_ready])
                for oc in range(NCH):
                    dd = [n2o, w2q_ops[-1]]
                    if cc == 0:
                        dd.append(conv_ops[-1])
                        dd.append(r_mu)
                        dd.append(r_ms)
                        if oc == 6:
                            dd.append(bi_ev)
                    y_mms[oc].append(S.add("tensor",
                                           lambda e, k=cc, o=oc, f=(cc == 0): e.matmul(
                        PS[o][:, 0:TTF], w2q[:, k * D + o * 128:k * D + (o + 1) * 128],
                        n2[:], start=f, stop=(k == 31)), deps=dd))
                prev_q2 = y_mms[7][-1]
            tile_outs = []
            prev_o = None
            for oc in range(NCH):
                ty = S.add("vector", lambda e, o=oc: e.tensor_mul(
                    tca[:], PS[o][:, 0:TTF], bcI[:]),
                    deps=[y_mms[oc][-1], bi_ev, prev_o])
                oo = S.add("vector", lambda e, o=oc: e.tensor_add(
                    osb[:], tca[:], x1f[:, o * W + 1:o * W + 1 + TTF]), deps=[ty])
                oap = dram_ap(outTh, oc * 128 * T + t0, [[T, 128], [1, TTF]])
                dm = S.add("sync", lambda e, a=oap: e.dma_start(
                    out=a, in_=osb[:]), deps=[oo], dma=True)
                prev_o = dm
                tile_outs.append(dm)
                out_dmas.append(dm)
            prev_consume = [y_mms[7][-1]] + tile_outs
            if debug and ti == 2:
                d1 = S.add("sync", lambda e: e.dma_start(out=dbg["dbg_n1"], in_=n1[:]),
                           deps=[y_mms[7][-1]], dma=True)
                d2 = S.add("sync", lambda e: e.dma_start(out=dbg["dbg_u"], in_=u_all[:]),
                           deps=[y_mms[7][-1]], dma=True)
                prev_consume = prev_consume + [d1, d2]
            prev_tile_out = tile_outs
            prev_rowfree = [bs_ev, bi_ev] + bc_ops
        S.barrier("sync", out_dmas)

        stF.close()
        S.emit(nc, sems)
    return nc, dbg


def prep_inputs(L0_attn_x, L0_attn_qkv_w, L0_attn_qkv_b, L0_attn_proj_w,
                L0_attn_proj_b, L0_ffn_w1, L0_ffn_conv_w, L0_ffn_conv_b,
                L0_ffn_w2):
    x = np.asarray(L0_attn_x, dtype=np.float32)
    qkv_w = np.asarray(L0_attn_qkv_w, dtype=np.float32)
    qkv_b = np.asarray(L0_attn_qkv_b, dtype=np.float32)
    proj_w = np.asarray(L0_attn_proj_w, dtype=np.float32)
    proj_b = np.asarray(L0_attn_proj_b, dtype=np.float32)
    w1 = np.asarray(L0_ffn_w1, dtype=np.float32)
    conv_w = np.asarray(L0_ffn_conv_w, dtype=np.float32)
    conv_b = np.asarray(L0_ffn_conv_b, dtype=np.float32)
    w2 = np.asarray(L0_ffn_w2, dtype=np.float32)

    shared = {
        "wqT": np.ascontiguousarray(qkv_w[0:D].T),
        "wkT": np.ascontiguousarray(qkv_w[D:2 * D].T),
        "wvT": np.ascontiguousarray(qkv_w[2 * D:3 * D].T),
        "qkb": np.ascontiguousarray(qkv_b[0:2 * D].reshape(16, 128).T),
        "vbrow": np.ascontiguousarray(qkv_b[2 * D:3 * D].reshape(1, D)),
        "projT": np.ascontiguousarray(proj_w.T),
        "pjb": np.ascontiguousarray(proj_b.reshape(8, 128).T),
        "w1T": np.ascontiguousarray(w1.T),
        "cw": np.ascontiguousarray(
            conv_w.reshape(32, 128, 3).transpose(1, 2, 0).reshape(128, 96)),
        "cbi": np.ascontiguousarray(conv_b.reshape(32, 128).T),
        "w2T": np.ascontiguousarray(w2.T),
    }
    in_maps = []
    for b in range(x.shape[0]):
        m = dict(shared)
        m["xT"] = np.ascontiguousarray(x[b].T)
        in_maps.append(m)
    return in_maps


def kernel(**inputs):
    in_maps = prep_inputs(**inputs)
    nc, _ = build()
    res = run_bass_kernel_spmd(nc, in_maps, core_ids=list(range(8)))
    out = np.empty((8, T, D), dtype=np.float32)
    for b in range(8):
        out[b] = np.asarray(res.results[b]["outT"]).T
    return out


if __name__ == "__main__":
    build()
    print("build ok")


# version 04
# speedup vs baseline: 1.2367x; 1.2367x over previous
import contextlib
import numpy as np

import concourse.bass as bass
import concourse.mybir as mybir
from concourse.bass_utils import run_bass_kernel_spmd

F32 = mybir.dt.float32
BF16 = mybir.dt.bfloat16
AF = mybir.ActivationFunctionType
ALU = mybir.AluOpType
AX = mybir.AxisListType.X

D = 1024
T = 4096
NH = 16
HD = 64
KT = 512
NKT = T // KT
QT = 512
HID = 4096
NCH = 8
TTF = 224
W = TTF + 2
ATT0 = T - QT

C23 = 12582912.0
C52 = 6755399441055744.0
BIG = 3.0e38
EPS_LN = 1e-5
SCALE = 0.125


class Op:
    __slots__ = ("eng", "fn", "deps", "is_dma", "tiny", "seq", "inc_val", "milestone")

    def __init__(self, eng, fn, deps, is_dma, tiny):
        self.eng, self.fn, self.deps, self.is_dma, self.tiny = eng, fn, deps, is_dma, tiny
        self.seq = -1
        self.inc_val = 0
        self.milestone = None


class Sched:
    ENGS = ("sync", "gpsimd", "vector", "scalar", "tensor")

    def __init__(self):
        self.ops = []
        self.gate = []

    def add(self, eng, fn, deps=(), dma=False, tiny=False):
        op = Op(eng, fn, [d for d in deps if d is not None] + self.gate, dma, tiny)
        op.seq = len(self.ops)
        self.ops.append(op)
        return op

    def barrier(self, eng, deps):
        return self.add(eng, None, deps)

    def phase_gate(self):
        tails = {}
        dma_idx = {}
        for op in self.ops:
            if op.is_dma:
                i = dma_idx.get(op.eng, 0)
                dma_idx[op.eng] = i + 1
                tails[(op.eng, "D%d" % (i % self.RR))] = op
            else:
                tails[(op.eng, "")] = op
        self.gate = []
        joins = [self.barrier(e, list(tails.values())) for e in self.ENGS]
        self.gate = joins
        return joins

    RR = 8

    def emit(self, nc, sems):
        needed = set()
        for op in self.ops:
            for d in op.deps:
                needed.add(d.seq)
        cnt = {}
        dma_idx = {}
        for op in self.ops:
            if op.is_dma:
                i = dma_idx.get(op.eng, 0)
                dma_idx[op.eng] = i + 1
                op.inc_val = 16
                op.milestone = ("%sD%d" % (op.eng, i % self.RR), 16 * (i // self.RR + 1))
            elif op.seq in needed:
                op.inc_val = 1
                c = cnt.get(op.eng, 0) + 1
                cnt[op.eng] = c
                op.milestone = (op.eng, c)
        per_eng = {e: [o for o in self.ops if o.eng == e] for e in self.ENGS}

        def emit_engine(eng_obj, eng_name):
            water = {}
            recent = []
            for op in per_eng[eng_name]:
                for d in op.deps:
                    if d.milestone is None:
                        raise RuntimeError("dep lacks milestone")
                    ch, val = d.milestone
                    if val > water.get(ch, 0):
                        water[ch] = val
                        eng_obj.wait_ge(sems[ch], val)
                if op.is_dma:
                    ch, val = op.milestone
                    if val > 16 and val - 16 > water.get(ch, 0):
                        water[ch] = val - 16
                        eng_obj.wait_ge(sems[ch], val - 16)
                if op.fn is not None:
                    inst = op.fn(eng_obj)
                    if op.milestone is not None:
                        ch, _ = op.milestone
                        inst.then_inc(sems[ch], op.inc_val)
                    recent.append(op)
                    if len(recent) > 3:
                        recent.pop(0)
                elif op.milestone is not None:
                    ch, _ = op.milestone
                    eng_obj.nop().then_inc(sems[ch], op.inc_val)

        with nc.Block() as blk:
            blk.sync(lambda e: emit_engine(e, "sync"))
            blk.gpsimd(lambda e: emit_engine(e, "gpsimd"))
            blk.vector(lambda e: emit_engine(e, "vector"))
            blk.scalar(lambda e: emit_engine(e, "scalar"))
            blk.tensor(lambda e: emit_engine(e, "tensor"))


def dram_ap(t, off, dims):
    return bass.AP(t, off, [list(d) for d in dims])


def build(debug=False):
    nc = bass.Bass("TRN2", target_bir_lowering=False)

    xTh = nc.dram_tensor("xT", [D, T], F32, kind="ExternalInput")
    wqTh = nc.dram_tensor("wqT", [D, D], F32, kind="ExternalInput")
    wkTh = nc.dram_tensor("wkT", [D, D], F32, kind="ExternalInput")
    wvTh = nc.dram_tensor("wvT", [D, D], F32, kind="ExternalInput")
    qkb = nc.dram_tensor("qkb", [128, 16], F32, kind="ExternalInput").ap()
    vbrow = nc.dram_tensor("vbrow", [1, D], F32, kind="ExternalInput").ap()
    projTh = nc.dram_tensor("projT", [D, D], F32, kind="ExternalInput")
    pjb = nc.dram_tensor("pjb", [128, 8], F32, kind="ExternalInput").ap()
    w1Th = nc.dram_tensor("w1T", [D, HID], F32, kind="ExternalInput")
    cw = nc.dram_tensor("cw", [128, 96], F32, kind="ExternalInput").ap()
    cbi = nc.dram_tensor("cbi", [128, 32], F32, kind="ExternalInput").ap()
    w2Th = nc.dram_tensor("w2T", [HID, D], F32, kind="ExternalInput")
    outTh = nc.dram_tensor("outT", [D, T], F32, kind="ExternalOutput")

    k_hbmh = nc.dram_tensor("k_hbm", [NH * 80, T], BF16, kind="Internal")
    k_hbm = k_hbmh.ap()
    v_hbmh = nc.dram_tensor("v_hbm", [T, D], BF16, kind="Internal")
    v_hbm = v_hbmh.ap()
    xattnh = nc.dram_tensor("xattn", [D, QT], F32, kind="Internal")

    dbg = {}
    if debug:
        for nm, shp, dt in [("dbg_xh", [128, 8 * KT], BF16),
                            ("dbg_lhist", [128, 576], F32),
                            ("dbg_mhist", [128, 576], F32),
                            ("dbg_b", [16, NH * QT], BF16),
                            ("dbg_ofm", [128, 8 * QT], BF16),
                            ("dbg_x1", [128, 8 * QT], F32)]:
            dbg[nm] = nc.dram_tensor(nm, shp, dt, kind="ExternalOutput").ap()

    S = Sched()
    st = contextlib.ExitStack()
    with st:
        def sb(name, shape, dt=F32, stk=st):
            return stk.enter_context(nc.sbuf_tensor(name, list(shape), dt))

        sems = {}
        for e in Sched.ENGS:
            sems[e] = st.enter_context(nc.semaphore(f"sem_{e}"))
            for i in range(Sched.RR):
                sems["%sD%d" % (e, i)] = st.enter_context(
                    nc.semaphore(f"sem_{e}D{i}"))

        onescol_b = sb("onescol_b", [128, 1], BF16)
        ones1_b = sb("ones1_b", [1, 128], BF16)
        ones1_f = sb("ones1_f", [1, 128], F32)
        onescol_f = sb("onescol_f", [128, 1], F32)
        identf = sb("identf", [128, 128], F32)
        identb = sb("identb", [128, 128], BF16)
        Ca = sb("Ca", [128, 1], F32)
        Ca1 = sb("Ca1", [1, 1], F32)
        qkb_s = sb("qkb_s", [128, 16], F32)
        qb8 = sb("qb8", [128, 8], F32)
        pjb_s = sb("pjb_s", [128, 8], F32)
        cw_s = sb("cw_s", [128, 96], F32)
        cb_s = sb("cb_s", [128, 32], F32)
        rowb = sb("rowb", [1, 2560], F32)
        c1_s = sb("c1_s", [1, 1], F32)
        c2_s = sb("c2_s", [1, 1], F32)
        rc1 = sb("rc1", [1, 1], F32)
        rc2 = sb("rc2", [1, 1], F32)
        sw1 = sb("sw1", [128, 1], F32)
        sw2 = sb("sw2", [128, 1], F32)
        stg4 = sb("stg4", [4, 128], F32)
        epsc = sb("epsc", [128, 1], F32)

        PS = [st.enter_context(nc.psum_tensor(f"PS{i}", [128, 512], F32)) for i in range(8)]

        R_MU, R_MSQ, R_VAR, R_SD, R_RSTD, R_T1 = 0, 512, 1024, 1536, 2048, 1536

        def row(off, w=KT):
            return rowb[0:1, off:off + w]

        stABCD = contextlib.ExitStack()
        stA = contextlib.ExitStack()
        stB = contextlib.ExitStack()
        stC = contextlib.ExitStack()
        stD = contextlib.ExitStack()

        q_aug = sb("q_aug", [80, NH * QT], BF16, stABCD)
        x7 = sb("x7", [128, 8 * QT], F32, stABCD)
        o_fm = sb("o_fm", [128, 8 * QT], BF16, stABCD)
        m_hist = sb("m_hist", [128, 64 * 9], F32, stABCD)
        l_hist = sb("l_hist", [128, 64 * 9], F32, stABCD)
        rmax_sc = sb("rmax_sc", [128, 64], F32, stABCD)
        sp_sc = sb("sp_sc", [128, 64], F32, stABCD)
        dsc = sb("dsc", [128, 64], F32, stABCD)
        desc = sb("desc", [128, 64], F32, stABCD)
        ltsc = sb("ltsc", [128, 64], F32, stABCD)
        scr = sb("scr", [128, 8 * KT], F32, stABCD)
        mkb = sb("mkb", [128, 4 * KT], F32, stABCD)

        wq_b = sb("wq_b", [128, 8 * D], BF16, stA)
        wk_b = sb("wk_b", [128, 8 * D], BF16, stA)
        wv_b = sb("wv_b", [128, 8 * D], BF16, stA)
        vb_b = sb("vb_b", [1, D], BF16, stA)
        ind8 = sb("ind8", [8, T], BF16, stA)
        xbuf = sb("xbuf", [128, 8 * KT], F32, stA)
        xscr = sb("xscr", [128, KT], F32, stA)
        xbb = sb("xbb", [128, 8 * KT], BF16, stA)
        xsq = sb("xsq", [128, KT], BF16, stA)
        xh = sb("xh", [128, 8 * KT], BF16, stA)
        kspill = sb("kspill", [128, 8 * KT], BF16, stA)
        vspill = sb("vspill", [128, 4 * D], BF16, stA)
        qstage = sb("qstage", [128, 8 * KT], BF16, stA)
        mu_b = sb("mu_b", [1, KT], BF16, stA)
        rs_b = sb("rs_b", [1, KT], BF16, stA)

        ld_small = S.add("sync", lambda e: e.dma_start(out=qkb_s[:], in_=qkb), dma=True)
        ld_pjb = S.add("sync", lambda e: e.dma_start(out=pjb_s[:], in_=pjb), dma=True)
        ld_cw = S.add("sync", lambda e: e.dma_start(out=cw_s[:], in_=cw), dma=True)
        ld_cb = S.add("sync", lambda e: e.dma_start(out=cb_s[:], in_=cbi), dma=True)
        ld_vb = S.add("sync", lambda e: e.dma_start(out=row(R_T1, D), in_=vbrow), dma=True)
        ld_w = {}
        for nm, src, dst in (("wq", wqTh, wq_b), ("wk", wkTh, wk_b), ("wv", wvTh, wv_b)):
            ap = dram_ap(src, 0, [[D, 128], [128 * D, 8], [1, D]])
            ld_w[nm] = S.add("gpsimd",
                             lambda e, a=ap, d=dst: e.dma_start(
                                 out=d[:].rearrange("p (c f) -> p c f", c=8), in_=a),
                             dma=True)

        c_on_b = S.add("vector", lambda e: e.memset(onescol_b[:], 1.0))
        c_o1b = S.add("vector", lambda e: e.memset(ones1_b[:], 1.0))
        c_o1f = S.add("vector", lambda e: e.memset(ones1_f[:], 1.0))
        c_onf = S.add("vector", lambda e: e.memset(onescol_f[:], 1.0))
        c_m0 = S.add("vector", lambda e: e.memset(m_hist[:, 0:64], -BIG))
        c_l0 = S.add("vector", lambda e: e.memset(l_hist[:, 0:64], 0.0))
        c_eps = S.add("vector", lambda e: e.memset(epsc[:], EPS_LN))

        g0 = S.add("gpsimd", lambda e: e.memset(identf[:], 1.0), deps=[ld_w["wv"]])
        g1 = S.add("gpsimd", lambda e: e.affine_select(
            identf[:], identf[:], [[-1, 128]], ALU.is_equal, 0.0,
            base=0, channel_multiplier=1), deps=[g0])
        identb_op = S.add("vector", lambda e: e.tensor_copy(identb[:], identf[:]),
                          deps=[g1])
        mk_ops = []
        for qc in range(4):
            a = S.add("gpsimd", lambda e, q=qc: e.memset(mkb[:, q * KT:(q + 1) * KT], BIG),
                      deps=[g1])
            b = S.add("gpsimd", lambda e, q=qc: e.affine_select(
                mkb[:, q * KT:(q + 1) * KT], mkb[:, q * KT:(q + 1) * KT],
                [[-1, KT]], ALU.is_ge, -BIG, base=128 * q, channel_multiplier=1),
                deps=[a])
            mk_ops.append(b)
        ia = S.add("gpsimd", lambda e: e.memset(ind8[:], 1.0))
        ib = S.add("gpsimd", lambda e: e.affine_select(
            ind8[:], ind8[:], [[1, T]], ALU.is_ge, 0.0,
            base=0, channel_multiplier=-KT), deps=[ia])
        ic = S.add("gpsimd", lambda e: e.affine_select(
            ind8[:], ind8[:], [[-1, T]], ALU.is_ge, 0.0,
            base=KT - 1, channel_multiplier=KT), deps=[ib])
        ind_dma = []
        for h in range(NH):
            for half in range(2):
                r0 = h * 80 + 64 + half * 8
                ind_dma.append(S.add("sync", lambda e, r=r0: e.dma_start(
                    out=k_hbm[r:r + 8, :], in_=ind8[:]), deps=[ic], dma=True))

        e0 = S.add("vector", lambda e: e.memset(Ca1[:], 2.0 ** -30), tiny=True)
        e1 = S.add("vector", lambda e: e.tensor_scalar(Ca1[:], Ca1[:], 1.0, None, ALU.add),
                   deps=[e0], tiny=True)
        e2 = S.add("vector", lambda e: e.tensor_scalar(
            Ca1[:], Ca1[:], -1.0, 2.0 ** 30, ALU.add, ALU.mult), deps=[e1], tiny=True)
        e3 = S.add("vector", lambda e: e.tensor_scalar(
            Ca1[:], Ca1[:], C52 - C23, C23, ALU.mult, ALU.add), deps=[e2], tiny=True)
        e4 = S.add("tensor", lambda e: e.matmul(
            PS[6][:, 0:1], ones1_f[:], Ca1[:], start=True, stop=True),
            deps=[e3, c_o1f])
        Ca_ready = S.add("scalar", lambda e: e.activation(Ca[:], PS[6][:, 0:1], AF.Identity),
                         deps=[e4], tiny=True)
        qb8_op = S.add("vector", lambda e: e.tensor_scalar(
            qb8[:], qkb_s[:, 0:8], 0.125, None, ALU.mult), deps=[ld_small], tiny=True)
        vb_op = S.add("vector", lambda e: e.tensor_copy(vb_b[:], row(R_T1, D)),
                      deps=[ld_vb])

        prev = {
            "xbuf": None, "xbb": None, "xsq": None, "xh": [],
            "ksp": [None] * (2 * NCH), "vsp": [None] * 4,
            "rMU": [], "rMSQ": [], "rT1": [], "rVAR": [], "rSD": [], "rRSTD": [],
            "mu_b": [], "rs_b": [], "ps45": [], "ps67": [],
        }
        kspill_dma, vspill_dma, qstage_dma = [], [], []
        x7_cp = None
        for t in range(NKT):
            xap = dram_ap(xTh, t * KT, [[T, 128], [128 * T, 8], [1, KT]])
            ldx = S.add("sync", lambda e, a=xap: e.dma_start(
                out=xbuf[:].rearrange("p (c f) -> p c f", c=8), in_=a),
                        deps=[prev["xbuf"]], dma=True)
            ldxb = S.add("gpsimd", lambda e, a=xap: e.dma_start(
                out=xbb[:].rearrange("p (c f) -> p c f", c=8), in_=a),
                         deps=[prev["xbb"]], dma=True)
            st_s = []
            for dc in range(NCH):
                st_s.append(S.add("tensor", lambda e, c=dc: e.matmul(
                    PS[4][0:1, :], onescol_b[:], xbb[:, c * KT:(c + 1) * KT],
                    start=(c == 0), stop=(c == NCH - 1)),
                    deps=[ldxb, c_on_b] + prev["ps45"]))
            st_q = []
            prev_sq = prev["xsq"]
            for dc in range(NCH):
                sq = S.add("scalar", lambda e, c=dc: e.activation(
                    xsq[:], xbuf[:, c * KT:(c + 1) * KT], AF.Square),
                    deps=[ldx, prev_sq])
                st_q.append(S.add("tensor", lambda e, c=dc: e.matmul(
                    PS[5][0:1, :], onescol_b[:], xsq[:],
                    start=(c == 0), stop=(c == NCH - 1)),
                    deps=[sq] + prev["ps45"]))
                prev_sq = st_q[-1]
            r_mu = S.add("scalar", lambda e: e.activation(
                row(R_MU), PS[4][0:1, :], AF.Identity, scale=1.0 / D),
                deps=[st_s[-1]] + prev["rMU"])
            r_ms = S.add("scalar", lambda e: e.activation(
                row(R_MSQ), PS[5][0:1, :], AF.Identity, scale=1.0 / D),
                deps=[st_q[-1]] + prev["rMSQ"])
            r_v1 = S.add("vector", lambda e: e.tensor_mul(
                row(R_T1), row(R_MU), row(R_MU)),
                deps=[r_mu, vb_op] + prev["rT1"] + prev["rSD"])
            r_v2 = S.add("vector", lambda e: e.tensor_sub(
                row(R_VAR), row(R_MSQ), row(R_T1)), deps=[r_ms, r_v1] + prev["rVAR"])
            r_sd = S.add("scalar", lambda e: e.activation(
                row(R_SD), row(R_VAR), AF.Sqrt, bias=epsc[0:1, 0:1]),
                deps=[r_v2, c_eps])
            r_rs = S.add("vector", lambda e: e.reciprocal(row(R_RSTD), row(R_SD)),
                         deps=[r_sd] + prev["rRSTD"])
            r_cb1 = S.add("vector", lambda e: e.tensor_copy(mu_b[:], row(R_MU)),
                          deps=[r_mu] + prev["mu_b"])
            r_cb2 = S.add("vector", lambda e: e.tensor_copy(rs_b[:], row(R_RSTD)),
                          deps=[r_rs] + prev["rs_b"])
            bc_mu, bc_rs = [], []
            for i in range(4):
                bc_mu.append(S.add("tensor", lambda e, j=i: e.matmul(
                    PS[6][:, j * 128:(j + 1) * 128], ones1_b[:],
                    mu_b[:, j * 128:(j + 1) * 128], start=True, stop=True),
                    deps=[r_cb1, c_o1b, Ca_ready] + prev["ps67"]))
                bc_rs.append(S.add("tensor", lambda e, j=i: e.matmul(
                    PS[7][:, j * 128:(j + 1) * 128], ones1_b[:],
                    rs_b[:, j * 128:(j + 1) * 128], start=True, stop=True),
                    deps=[r_cb2] + prev["ps67"]))
            xh_ops = []
            prev_sub = None
            for dc in range(NCH):
                aa = S.add("vector", lambda e, c=dc: e.tensor_sub(
                    xscr[:], xbuf[:, c * KT:(c + 1) * KT], PS[6][:]),
                    deps=bc_mu + [ldx, prev_sub] + (prev["xh"] if dc == 0 else []))
                bb_ = S.add("vector", lambda e, c=dc: e.tensor_mul(
                    xh[:, c * KT:(c + 1) * KT], xscr[:], PS[7][:]),
                    deps=bc_rs + [aa])
                prev_sub = bb_
                xh_ops.append(bb_)
            prev_ev = [None, None]
            kevac = []
            for kc in range(NCH):
                slot = kc % 2
                mms = []
                for dc in range(NCH):
                    mms.append(S.add("tensor", lambda e, c=dc, k=kc, s=slot: e.matmul(
                        PS[s][:], wk_b[:, c * D + k * 128: c * D + (k + 1) * 128],
                        xh[:, c * KT:(c + 1) * KT],
                        start=(c == 0), stop=(c == NCH - 1)),
                        deps=[xh_ops[dc], ld_w["wk"], prev_ev[slot]]))
                ev = S.add("scalar", lambda e, k=kc, s=slot: e.activation(
                    kspill[:, k * KT:(k + 1) * KT], PS[s][:], AF.Identity,
                    bias=qkb_s[:, 8 + k:9 + k]),
                    deps=[mms[-1], ld_small, prev["ksp"][2 * kc],
                          prev["ksp"][2 * kc + 1]])
                prev_ev[slot] = ev
                kevac.append(ev)
            ksp = []
            for kc in range(NCH):
                for hh in range(2):
                    dstap = dram_ap(k_hbmh, ((2 * kc + hh) * 80) * T + t * KT,
                                    [[T, 64], [1, KT]])
                    srcap = kspill[hh * 64:(hh + 1) * 64, kc * KT:(kc + 1) * KT]
                    ksp.append(S.add("sync", lambda e, dd=dstap, ss=srcap: e.dma_start(
                        out=dd, in_=ss), deps=[kevac[kc]], dma=True))
            kspill_dma = ksp
            prev_vev = [None, None]
            vevac = {}
            for tc in range(4):
                for nh2 in range(2):
                    slot = 2 + (2 * tc + nh2) % 2
                    mms = []
                    for dc in range(NCH):
                        mms.append(S.add("tensor",
                                         lambda e, c=dc, tc_=tc, n=nh2, s=slot: e.matmul(
                            PS[s][:],
                            xh[:, c * KT + tc_ * 128: c * KT + (tc_ + 1) * 128],
                            wv_b[:, c * D + n * 512:c * D + (n + 1) * 512],
                            start=(c == 0), stop=False),
                            deps=[xh_ops[dc], ld_w["wv"], prev_vev[slot - 2]]))
                    mb = S.add("tensor", lambda e, n=nh2, s=slot: e.matmul(
                        PS[s][:], ones1_b[:], vb_b[0:1, n * 512:(n + 1) * 512],
                        start=False, stop=True), deps=[vb_op, mms[-1]])
                    ev = S.add("scalar", lambda e, tc_=tc, n=nh2, s=slot: e.activation(
                        vspill[:, tc_ * D + n * 512: tc_ * D + (n + 1) * 512],
                        PS[s][:], AF.Identity), deps=[mb, prev["vsp"][tc]])
                    prev_vev[slot - 2] = ev
                    vevac[(tc, nh2)] = ev
            vsp = []
            for tc in range(4):
                dst = v_hbm[t * KT + tc * 128: t * KT + (tc + 1) * 128, :]
                vsp.append(S.add("gpsimd", lambda e, dd=dst, tc_=tc: e.dma_start(
                    out=dd, in_=vspill[:, tc_ * D:(tc_ + 1) * D]),
                    deps=[vevac[(tc, 0)], vevac[(tc, 1)]], dma=True))
            vspill_dma = vsp
            q_last = []
            if t == 7:
                prev_q = [None, None]
                qevac = []
                for qc in range(NCH):
                    slot = qc % 2
                    mms = []
                    for dc in range(NCH):
                        mms.append(S.add("tensor", lambda e, c=dc, k=qc, s=slot: e.matmul(
                            PS[s][:], wq_b[:, c * D + k * 128: c * D + (k + 1) * 128],
                            xh[:, c * KT:(c + 1) * KT],
                            start=(c == 0), stop=(c == NCH - 1)),
                            deps=[xh_ops[dc], ld_w["wq"],
                                  prev_q[slot] if prev_q[slot] else kevac[slot + 6]]))
                    ev = S.add("scalar", lambda e, k=qc, s=slot: e.activation(
                        qstage[:, k * KT:(k + 1) * KT], PS[s][:], AF.Identity,
                        scale=SCALE, bias=qb8[:, k:k + 1]), deps=[mms[-1], qb8_op])
                    prev_q[slot] = ev
                    qevac.append(ev)
                    q_last = mms
                for qc in range(NCH):
                    for hh in range(2):
                        h = 2 * qc + hh
                        src = qstage[hh * 64:(hh + 1) * 64, qc * KT:(qc + 1) * KT]
                        qstage_dma.append(S.add("gpsimd",
                                               lambda e, ss=src, h_=h: e.dma_start(
                            out=q_aug[0:64, h_ * QT:(h_ + 1) * QT], in_=ss),
                            deps=[qevac[qc]], dma=True))
                x7_cp = S.add("gpsimd", lambda e: e.dma_start(
                    out=x7[:], in_=xbuf[:]), deps=[ldx, xh_ops[-1]], dma=True)
            prev["xbuf"] = x7_cp if t == 7 else xh_ops[-1]
            prev["xbb"] = st_s[-1]
            prev["xsq"] = st_q[-1]
            prev["xh"] = ([q_last[-1]] if q_last else []) + [mms[-1]]
            prev["ksp"] = ksp
            prev["vsp"] = vsp
            prev["rMU"] = [r_v1, r_cb1]
            prev["rMSQ"] = [r_v2]
            prev["rT1"] = [r_v2]
            prev["rVAR"] = [r_sd]
            prev["rSD"] = [r_rs]
            prev["rRSTD"] = [r_cb2]
            prev["mu_b"] = bc_mu
            prev["rs_b"] = bc_rs
            prev["ps45"] = [r_mu, r_ms]
            prev["ps67"] = xh_ops
        xh_last = prev["xh"]

        stA.close()
        qdup = sb("qdup", [128, NH * QT], BF16, stB)
        kdup = sb("kdup", [128, NH * KT], BF16, stB)
        psink = sb("psink", [128, 8 * KT], BF16, stB)
        qd = []
        for half in range(2):
            qd.append(S.add("gpsimd", lambda e, hh=half: e.dma_start(
                out=qdup[hh * 64:(hh + 1) * 64, :], in_=q_aug[0:64, :]),
                deps=qstage_dma, dma=True))
        S.phase_gate()
        sfx = sb("sfx", [128, 128], F32, stB)
        rl5 = sb("rl5", [128, 512], F32, stB)
        rr5 = sb("rr5", [128, 512], F32, stB)
        gg5 = sb("gg5", [128, 512], F32, stB)
        wh5 = sb("wh5", [128, 512], F32, stB)
        bb5 = sb("bb5", [128, 512], F32, stB)
        bb5b = sb("bb5b", [128, 512], F32, stB)
        b1c = sb("b1c", [128, 512], BF16, stB)
        b2c = sb("b2c", [128, 512], BF16, stB)
        b2f = sb("b2f", [128, 512], F32, stB)
        lnlt = sb("lnlt", [128, 512], F32, stB)
        bstg = sb("bstg", [32, 256], BF16, stB)
        bh1 = sb("bh1", [128, 512], F32, stB)
        bh2 = sb("bh2", [128, 512], F32, stB)
        mk1 = mkb

        s_free = [None] * 8
        psink_prev = [None] * 8
        prev_kd = None
        s5_prev = [None] * 8
        s4_prev = [None] * 8
        s5_last = None
        for j in range(NKT):
            lds = []
            for half in range(2):
                ap = dram_ap(k_hbmh, j * KT, [[T, 64], [80 * T, NH], [1, KT]])
                lds.append(S.add("sync", lambda e, a=ap, hh=half: e.dma_start(
                    out=kdup[hh * 64:(hh + 1) * 64, :].rearrange(
                        "p (c f) -> p c f", c=NH), in_=a),
                    deps=[prev_kd] if prev_kd else list(kspill_dma) + qd,
                    dma=True))
            last_mm = None
            for a in range(8):
                tile8 = []
                for r2 in range(2):
                    for sub in range(2):
                        qc = 2 * r2 + sub
                        for hh in range(2):
                            h = 2 * a + hh
                            hq = h * 4 + qc
                            slot = 4 * r2 + 2 * sub + hh
                            mm = S.add("tensor",
                                       lambda e, h_=h, q_=qc, s=slot, hh_=hh: e.matmul(
                                PS[s][:],
                                qdup[hh_ * 64:(hh_ + 1) * 64,
                                     h_ * QT + q_ * 128: h_ * QT + (q_ + 1) * 128],
                                kdup[hh_ * 64:(hh_ + 1) * 64, h_ * KT:(h_ + 1) * KT],
                                start=True, stop=True, tile_position=(64 * hh_, 0)),
                                deps=[lds[hh], qd[hh], s_free[slot]])
                            last_mm = mm
                            if j == 7:
                                msrc = S.add("vector",
                                             lambda e, s=slot, q_=qc: e.tensor_tensor(
                                    scr[:, s * KT:(s + 1) * KT], PS[s][:],
                                    mk1[:, q_ * KT:(q_ + 1) * KT], ALU.min),
                                    deps=[mm] + mk_ops)
                            else:
                                msrc = mm
                            rd = S.add("vector", lambda e, s=slot, c=hq, jj=j: e.reduce_max(
                                rmax_sc[:, c:c + 1],
                                (scr[:, s * KT:(s + 1) * KT] if jj == 7 else PS[s][:]),
                                axis=AX), deps=[msrc], tiny=True)
                            tile8.append((hq, slot, msrc, rd))
                c0 = 8 * a
                s1 = S.add("vector", lambda e, c=c0, jj=j: e.tensor_max(
                    m_hist[:, (jj + 1) * 64 + c:(jj + 1) * 64 + c + 8],
                    m_hist[:, jj * 64 + c:jj * 64 + c + 8],
                    rmax_sc[:, c:c + 8]),
                    deps=[x[3] for x in tile8] + ([c_m0] if j == 0 else []), tiny=True)
                s1n = S.add("vector", lambda e, c=c0, jj=j: e.tensor_scalar(
                    dsc[:, c:c + 8],
                    m_hist[:, (jj + 1) * 64 + c:(jj + 1) * 64 + c + 8],
                    -1.0, None, ALU.mult), deps=[s1], tiny=True)
                exps = []
                for (hq, slot, msrc, rd) in tile8:
                    ex = S.add("scalar", lambda e, s=slot, c=hq, jj=j: e.activation(
                        psink[:, s * KT:(s + 1) * KT],
                        (scr[:, s * KT:(s + 1) * KT] if jj == 7 else PS[s][:]),
                        AF.Exp, bias=dsc[:, c:c + 1], accum_out=sp_sc[:, c:c + 1]),
                        deps=[s1n, msrc, s5_prev[a], psink_prev[slot]])
                    exps.append(ex)
                    s_free[slot] = ex
                    psink_prev[slot] = ex
                s2 = S.add("vector", lambda e, c=c0, jj=j: e.tensor_sub(
                    dsc[:, c:c + 8], m_hist[:, jj * 64 + c:jj * 64 + c + 8],
                    m_hist[:, (jj + 1) * 64 + c:(jj + 1) * 64 + c + 8]),
                    deps=[s1, exps[-1]], tiny=True)
                s3 = S.add("scalar", lambda e, c=c0: e.activation(
                    desc[:, c:c + 8], dsc[:, c:c + 8], AF.Exp),
                    deps=[s2, s4_prev[a]], tiny=True)
                s4 = S.add("vector", lambda e, c=c0, jj=j: e.tensor_mul(
                    ltsc[:, c:c + 8], l_hist[:, jj * 64 + c:jj * 64 + c + 8],
                    desc[:, c:c + 8]),
                    deps=[s3] + ([c_l0] if j == 0 else []), tiny=True)
                s5 = S.add("vector", lambda e, c=c0, jj=j: e.tensor_add(
                    l_hist[:, (jj + 1) * 64 + c:(jj + 1) * 64 + c + 8],
                    ltsc[:, c:c + 8], sp_sc[:, c:c + 8]),
                    deps=[s4] + exps, tiny=True)
                s5_prev[a] = s5
                s4_prev[a] = s4
                s5_last = s5
            prev_kd = last_mm

        pm1 = S.add("vector", lambda e: e.reciprocal(rl5[:], l_hist[:, 64:576]),
                    deps=[s5_last])
        pm2 = S.add("vector", lambda e: e.tensor_mul(rr5[:], l_hist[:, 0:512], rl5[:]),
                    deps=[pm1])
        pm3 = S.add("vector", lambda e: e.tensor_scalar(
            gg5[:], rr5[:], -1.0, 1.0, ALU.mult, ALU.add), deps=[pm2])
        sfx_prev = S.add("vector", lambda e: e.memset(sfx[:, 0:64], 1.0), deps=[pm3])
        wops = []
        for jj in range(7, -1, -1):
            cur = jj % 2
            w_ = S.add("vector", lambda e, j_=jj, c=(jj + 1) % 2: e.tensor_mul(
                wh5[:, j_ * 64:(j_ + 1) * 64], rr5[:, j_ * 64:(j_ + 1) * 64],
                sfx[:, c * 64:(c + 1) * 64]), deps=[sfx_prev, pm2])
            if jj > 0:
                sfx_prev = S.add("vector", lambda e, j_=jj, c=(jj + 1) % 2, n=jj % 2:
                                 e.tensor_mul(
                    sfx[:, n * 64:(n + 1) * 64], sfx[:, c * 64:(c + 1) * 64],
                    gg5[:, j_ * 64:(j_ + 1) * 64]), deps=[w_, pm3])
            wops.append(w_)
        pw1 = S.add("vector", lambda e: e.tensor_scalar(
            wh5[:], wh5[:], 1e-38, None, ALU.max), deps=wops)
        pw2 = S.add("scalar", lambda e: e.activation(bb5[:], wh5[:], AF.Ln), deps=[pw1])
        pl1 = S.add("scalar", lambda e: e.activation(
            lnlt[:, 0:64], l_hist[:, 512:576], AF.Ln), deps=[pm1])
        tile_l = [pl1]
        for jj in range(1, 8):
            tile_l.append(S.add("vector", lambda e, j_=jj: e.tensor_copy(
                lnlt[:, j_ * 64:(j_ + 1) * 64], lnlt[:, 0:64]), deps=[pl1]))
        pb1 = S.add("vector", lambda e: e.tensor_sub(bb5b[:], bb5[:], lnlt[:]),
                    deps=[pw2] + tile_l)
        pb2 = S.add("vector", lambda e: e.tensor_sub(bb5[:], bb5b[:], m_hist[:, 64:576]),
                    deps=[pb1])
        pb3 = S.add("vector", lambda e: e.tensor_scalar(
            bb5b[:], bb5[:], -80.0, 80.0, ALU.max, ALU.min), deps=[pb2])
        pb4 = S.add("vector", lambda e: e.tensor_copy(b1c[:], bb5b[:]), deps=[pb3])
        pb5 = S.add("vector", lambda e: e.tensor_sub(b2f[:], bb5b[:], b1c[:]), deps=[pb4])
        pb6 = S.add("vector", lambda e: e.tensor_copy(b2c[:], b2f[:]), deps=[pb5])
        bstage_dma = []
        prev_tr = None
        reord = []
        for h in range(NH):
            for qc in range(4):
                hq = h * 4 + qc
                reord.append(S.add("vector", lambda e, q_=hq: e.tensor_copy(
                    bh1[:, q_ * 8:(q_ + 1) * 8],
                    bass.AP(bb5b, q_, [[512, 128], [64, 8]])), deps=[pb3]))
                reord.append(S.add("vector", lambda e, q_=hq: e.tensor_copy(
                    bh2[:, q_ * 8:(q_ + 1) * 8],
                    bass.AP(b2f, q_, [[512, 128], [64, 8]])), deps=[pb5]))
        prev_sdma = [None, None]
        for h in range(NH):
            for qc in range(4):
                hq = h * 4 + qc
                tr1 = S.add("tensor", lambda e, q_=hq: e.transpose(
                    PS[4][0:8, 0:128], bh1[:, q_ * 8:(q_ + 1) * 8], identf[:]),
                    deps=[reord[2 * hq], identb_op, prev_tr, s_free[4]])
                ev1 = S.add("scalar", lambda e: e.activation(
                    bstg[0:8, 0:128], PS[4][0:8, 0:128], AF.Identity),
                    deps=[tr1, prev_sdma[0]])
                tr2 = S.add("tensor", lambda e, q_=hq: e.transpose(
                    PS[5][0:8, 0:128], bh2[:, q_ * 8:(q_ + 1) * 8], identf[:]),
                    deps=[reord[2 * hq + 1], ev1, s_free[5]])
                ev2 = S.add("scalar", lambda e: e.activation(
                    bstg[0:8, 128:256], PS[5][0:8, 0:128], AF.Identity),
                    deps=[tr2, prev_sdma[1]])
                prev_tr = ev2
                for which, ev in ((0, ev1), (1, ev2)):
                    dstap = q_aug[64 + which * 8:72 + which * 8,
                                  h * QT + qc * 128: h * QT + (qc + 1) * 128]
                    srcap = bstg[0:8, which * 128:(which + 1) * 128]
                    dmop = S.add("gpsimd", lambda e, dd=dstap, ss=srcap: e.dma_start(
                        out=dd, in_=ss), deps=[ev], dma=True)
                    bstage_dma.append(dmop)
                    prev_sdma[which] = dmop

        if debug:
            S.add("sync", lambda e: e.dma_start(
                out=dbg["dbg_lhist"], in_=l_hist[:, 0:576]), deps=[pm2], dma=True)
            S.add("sync", lambda e: e.dma_start(
                out=dbg["dbg_mhist"], in_=m_hist[:, 0:576]), deps=[pb3], dma=True)
            S.add("sync", lambda e: e.dma_start(
                out=dbg["dbg_b"], in_=q_aug[64:80, :]),
                deps=list(bstage_dma), dma=True)
        S.phase_gate()
        stB.close()
        kaug = sb("kaug", [80, NH * KT], BF16, stC)
        o_acc = [sb("o_accA", [128, 8 * KT], F32, stC),
                 sb("o_accB", [128, 8 * KT], F32, stC)]
        vtile = sb("vtile", [128, 4 * D], BF16, stC)
        ppb = sb("ppb", [128, 4 * KT], BF16, stC)
        mk2 = mkb
        mk2_ops = []
        for c_ in range(4):
            a_ = S.add("gpsimd", lambda e, q=c_: e.memset(
                mkb[:, q * KT:(q + 1) * KT], BIG), deps=[s5_last] + list(bstage_dma))
            b_ = S.add("gpsimd", lambda e, q=c_: e.affine_select(
                mkb[:, q * KT:(q + 1) * KT], mkb[:, q * KT:(q + 1) * KT],
                [[1, KT]], ALU.is_ge, -BIG, base=-128 * q, channel_multiplier=-1),
                deps=[a_])
            mk2_ops.append(b_)
        prev_ka = None
        prev_vt = None
        o_cast = {}
        pp_free = None
        pv_bank_free = [None] * 4
        oacc_prev = {}
        for grp in range(2):
            for j in range(NKT):
                kap = dram_ap(k_hbmh, j * KT, [[T, 80], [80 * T, NH], [1, KT]])
                lk = S.add("sync", lambda e, a=kap: e.dma_start(
                    out=kaug[:].rearrange("p (c f) -> p c f", c=NH), in_=a),
                    deps=([prev_ka] if prev_ka else list(bstage_dma) + ind_dma),
                    dma=True)
                vap = dram_ap(v_hbmh, j * KT * D, [[D, 128], [128 * D, 4], [1, D]])
                lv = S.add("sync", lambda e, a=vap: e.dma_start(
                    out=vtile[:].rearrange("p (c f) -> p c f", c=4), in_=a),
                    deps=([prev_vt] if prev_vt else list(vspill_dma)),
                    dma=True)
                for hh in range(8):
                    h = grp * 8 + hh
                    pair = hh // 2
                    chunk = grp * 4 + pair
                    slot = 4 + pair
                    smm = []
                    for c in range(4):
                        smm.append(S.add("tensor", lambda e, h_=h, c_=c: e.matmul(
                            PS[c_][:],
                            kaug[:, h_ * KT + c_ * 128: h_ * KT + (c_ + 1) * 128],
                            q_aug[:, h_ * QT:(h_ + 1) * QT],
                            start=True, stop=True),
                            deps=[lk, s_free[c_]] + list(bstage_dma)))
                    if j == 7:
                        pre = []
                        for c in range(4):
                            pre.append(S.add("vector", lambda e, c_=c: e.tensor_tensor(
                                scr[:, c_ * KT:(c_ + 1) * KT], PS[c_][:],
                                mk2[:, c_ * KT:(c_ + 1) * KT], ALU.min),
                                deps=[smm[c]] + mk2_ops))
                        exd = pre
                    else:
                        exd = smm
                    exps = []
                    for c in range(4):
                        ex = S.add("scalar", lambda e, c_=c, jj=j: e.activation(
                            ppb[:, c_ * KT:(c_ + 1) * KT],
                            (scr[:, c_ * KT:(c_ + 1) * KT] if jj == 7 else PS[c_][:]),
                            AF.Exp),
                            deps=[exd[c]] + ([pp_free] if (pp_free and c == 0) else []))
                        exps.append(ex)
                        s_free[c] = ex
                    pvs = []
                    for c in range(4):
                        pvs.append(S.add("tensor",
                                         lambda e, h_=h, c_=c, s=slot: e.matmul(
                            PS[s][64 * (h_ % 2):64 * (h_ % 2) + 64, :],
                            vtile[:, c_ * D + h_ * HD: c_ * D + (h_ + 1) * HD],
                            ppb[:, c_ * KT:(c_ + 1) * KT],
                            start=(c_ == 0), stop=(c_ == 3),
                            tile_position=(0, 64 * (h_ % 2))),
                            deps=[exps[c_], lv, pv_bank_free[pair]]))
                    pp_free = pvs[-1]
                    prev_ka = smm[-1]
                    prev_vt = pvs[-1]
                    if h % 2 == 1:
                        src_b = j % 2
                        dst_b = (j + 1) % 2
                        if j == 0:
                            acc = S.add("vector", lambda e, ch=chunk, s=slot: e.tensor_copy(
                                o_acc[1][:, ch % 4 * KT:(ch % 4 + 1) * KT], PS[s][:]),
                                deps=pvs + [oacc_prev.get((1, chunk))])
                        else:
                            acc = S.add("vector", lambda e, ch=chunk, s=slot,
                                        sb_=src_b, db_=dst_b: e.tensor_add(
                                o_acc[db_][:, ch % 4 * KT:(ch % 4 + 1) * KT],
                                o_acc[sb_][:, ch % 4 * KT:(ch % 4 + 1) * KT],
                                PS[s][:]),
                                deps=pvs + [oacc_prev.get((dst_b, chunk))])
                        oacc_prev[((j + 1) % 2, chunk)] = acc
                        pv_bank_free[pair] = acc
                        if j == NKT - 1:
                            o_cast[chunk] = S.add("vector",
                                                  lambda e, ch=chunk: e.tensor_copy(
                                o_fm[:, ch * KT:(ch + 1) * KT],
                                o_acc[0][:, ch % 4 * KT:(ch % 4 + 1) * KT]),
                                deps=[acc])

        S.phase_gate()
        stC.close()
        pj_b = sb("pj_b", [128, 8 * D], BF16, stD)
        xat_sb = sb("xat_sb", [128, 8 * QT], F32, stD)
        ld_pj = S.add("gpsimd", lambda e: e.dma_start(
            out=pj_b[:].rearrange("p (c f) -> p c f", c=8),
            in_=dram_ap(projTh, 0, [[D, 128], [128 * D, 8], [1, D]])),
            dma=True)
        prev_pd = [None, None]
        xat_ops = []
        for oc in range(NCH):
            slot = oc % 2
            mms = []
            for dc in range(NCH):
                mms.append(S.add("tensor", lambda e, c=dc, o=oc, s=slot: e.matmul(
                    PS[s][:], pj_b[:, c * D + o * 128: c * D + (o + 1) * 128],
                    o_fm[:, c * QT:(c + 1) * QT],
                    start=(c == 0), stop=(c == NCH - 1)),
                    deps=[ld_pj, o_cast[c], prev_pd[slot],
                          s_free[slot] if oc < 2 else None]))
            ev = S.add("vector", lambda e, o=oc, s=slot: e.tensor_add(
                xat_sb[:, o * QT:(o + 1) * QT], PS[s][:], x7[:, o * QT:(o + 1) * QT]),
                deps=[mms[-1], x7_cp])
            prev_pd[slot] = ev
            xat_ops.append(ev)
        xat_dma = S.add("sync", lambda e: e.dma_start(
            out=dram_ap(xattnh, 0, [[QT, 128], [128 * QT, 8], [1, QT]]),
            in_=xat_sb[:].rearrange("p (c f) -> p c f", c=8)),
            deps=xat_ops, dma=True)

        if debug:
            S.add("sync", lambda e: e.dma_start(out=dbg["dbg_ofm"], in_=o_fm[:]),
                  deps=list(o_cast.values()), dma=True)
            S.add("sync", lambda e: e.dma_start(out=dbg["dbg_x1"], in_=xat_sb[:]),
                  deps=xat_ops, dma=True)
            S.add("sync", lambda e: e.dma_start(out=dbg["dbg_xh"], in_=xh[:]),
                  deps=xh_last, dma=True)

        joins = S.phase_gate()
        stD.close()
        stABCD.close()

        stF = contextlib.ExitStack()

        def sbF(name, shape, dt=F32):
            return sb(name, shape, dt, stF)

        w1q = sbF("w1q", [128, 8 * HID], BF16)
        w2q = sbF("w2q", [128, 32 * D], BF16)
        asum = sbF("asum", [128, 40], F32)
        stW = contextlib.ExitStack()
        wstr = [sb("wstr0", [128, HID], F32, stW), sb("wstr1", [128, HID], F32, stW)]

        def tern(whbm, nrows, wq_dst, colw, sw_col, cval, rscratch, first_deps):
            absops = []
            lds = [None, None]
            for r in range(nrows):
                b = r % 2
                ap = dram_ap(whbm, r * 128 * colw, [[colw, 128], [1, colw]])
                ld = S.add("sync", lambda e, a=ap, bb=b: e.dma_start(
                    out=wstr[bb][:, 0:colw], in_=a),
                    deps=([absops[r - 2]] if r >= 2 else list(first_deps)), dma=True)
                ab = S.add("scalar", lambda e, rr=r, n=colw, bb=b: e.activation(
                    wstr[bb][:, 0:n], wstr[bb][:, 0:n], AF.Abs,
                    accum_out=asum[:, rr:rr + 1]), deps=[ld])
                absops.append(ab)
            red = S.add("vector", lambda e, n=nrows: e.tensor_reduce(
                asum[:, 39:40], asum[:, 0:n], axis=AX, op=ALU.add),
                deps=absops, tiny=True)
            mm = S.add("tensor", lambda e: e.matmul(
                PS[6][0:1, 0:1], onescol_f[:], asum[:, 39:40], start=True, stop=True),
                deps=[red, c_onf])
            sc1 = S.add("scalar", lambda e, n=nrows, cwid=colw: e.activation(
                cval[:], PS[6][0:1, 0:1], AF.Identity, scale=1.0 / (n * 128 * cwid)),
                deps=[mm], tiny=True)
            sc2 = S.add("vector", lambda e: e.tensor_scalar(
                cval[:], cval[:], 1e-5, None, ALU.max), deps=[sc1], tiny=True)
            sc3 = S.add("vector", lambda e: e.reciprocal(rscratch[:], cval[:]),
                        deps=[sc2], tiny=True)
            bc = S.add("tensor", lambda e: e.matmul(
                PS[6][:, 0:1], ones1_f[:], rscratch[:], start=True, stop=True),
                deps=[sc3])
            bce = S.add("scalar", lambda e: e.activation(
                sw_col[:], PS[6][:, 0:1], AF.Identity), deps=[bc], tiny=True)
            outs = []
            t3s = [bce, bce]
            for r in range(nrows):
                b = r % 2
                ap = dram_ap(whbm, r * 128 * colw, [[colw, 128], [1, colw]])
                ld = S.add("sync", lambda e, a=ap, bb=b: e.dma_start(
                    out=wstr[bb][:, 0:colw], in_=a), deps=[t3s[b], absops[-1]],
                    dma=True)
                t1 = S.add("vector", lambda e, n=colw, bb=b: e.tensor_scalar(
                    wstr[bb][:, 0:n], wstr[bb][:, 0:n], sw_col[:], None, ALU.mult),
                    deps=[ld, bce])
                t2 = S.add("vector", lambda e, n=colw, bb=b: e.tensor_scalar(
                    wstr[bb][:, 0:n], wstr[bb][:, 0:n], Ca[:], Ca[:],
                    ALU.add, ALU.subtract), deps=[t1, Ca_ready])
                t3 = S.add("vector", lambda e, rr=r, n=colw, bb=b: e.tensor_scalar(
                    wq_dst[:, rr * n:(rr + 1) * n], wstr[bb][:, 0:n],
                    1.0, -1.0, ALU.min, ALU.max), deps=[t2])
                t3s[b] = t3
                outs.append(t3)
            return outs, sc2, bce

        w1q_ops, c1_op, w1q_bce = tern(w1Th, NCH, w1q, HID, sw1, c1_s, rc1, joins)
        w2q_ops, c2_op, w2q_bce = tern(w2Th, 32, w2q, D, sw2, c2_s, rc2,
                                       [w1q_ops[-1], w1q_ops[-2]])
        S.phase_gate()
        stW.close()

        u_all = sbF("u_all", [128, 32 * TTF], F32)
        xload = [sbF("xload0", [128, 8 * W], F32), sbF("xload1", [128, 8 * W], F32)]
        x1b = [sbF("x1b0", [128, 8 * W], BF16), sbF("x1b1", [128, 8 * W], BF16)]
        n1 = x1b
        xqb = sbF("xqb", [128, 8 * W], BF16)
        x1c = sbF("x1c", [128, W], F32)
        mxb = sbF("mxb", [128, 256], F32)
        mnb = sbF("mnb", [128, 256], F32)
        m2b = sbF("m2b", [128, 256], F32)
        bcA = sbF("bcA", [128, W], F32)
        bcB = sbF("bcB", [128, W], F32)
        bcC = sbF("bcC", [128, W], F32)
        bcS = sbF("bcS", [128, TTF], F32)
        bcI = sbF("bcI", [128, TTF], F32)
        tca = [sbF("tca0", [128, TTF], F32), sbF("tca1", [128, TTF], F32)]
        tcb = sbF("tcb", [128, TTF], F32)
        n2g = sbF("n2g", [128, 2 * TTF], BF16)
        osb = [sbF("osb0", [128, TTF], F32), sbF("osb1", [128, TTF], F32)]

        H_MU, H_MSQ, H_VAR, H_RSTD, H_MX, H_MN, T_S2, T_I2 = (
            0, 256, 512, 768, 1024, 1280, 1536, 1792)
        SOF = 256

        FF_TILES = []
        tt = 0
        while tt < T:
            FF_TILES.append((tt, min(TTF, T - tt)))
            tt += TTF
        NT = len(FF_TILES)

        pr = {
            "xl": [[], []], "x1b": [[], []], "xqb": [],
            "rows_h": [w1q_bce, w2q_bce], "rows_t": [w1q_bce, w2q_bce],
            "ps45": [], "ps67": [], "psy": [None] * 8, "osb": [None, None],
            "mx": [], "mn": [], "m2": [], "stg": [], "bcH": [], "bcT": [],
            "n2g": [None] * 2, "ch": None, "tcab": [],
        }
        ps67_rr = [0]
        out_dmas = []
        TS = {}

        def cmax(buf, dep_ops, use_min, rowoff, wpad):
            nblk = (wpad + 127) // 128
            rds = []
            for i4 in range(nblk):
                bank = 6 + (ps67_rr[0] % 2)
                ps67_rr[0] += 1
                tr = S.add("tensor", lambda e, i=i4, b=buf, bk=bank: e.transpose(
                    PS[bk][:, SOF:SOF + 128], b[:, i * 128:(i + 1) * 128],
                    identf[:]),
                    deps=list(dep_ops) + [identb_op] + pr["ps67"] +
                         ([rds[-2]] if len(rds) >= 2 else []))
                rd = S.add("vector", lambda e, i=i4, m=use_min, bk=bank: e.tensor_reduce(
                    asum[:, 32 + i:33 + i], PS[bk][:, SOF:SOF + 128], axis=AX,
                    op=(ALU.min if m else ALU.max)), deps=[tr], tiny=True)
                rds.append(rd)
            bank = 6 + (ps67_rr[0] % 2)
            ps67_rr[0] += 1
            tr2 = S.add("tensor", lambda e, n=nblk, bk=bank: e.transpose(
                PS[bk][0:n, SOF + 128:SOF + 256], asum[:, 32:32 + n], identf[:]),
                deps=rds + pr["ps67"])
            ev = S.add("scalar", lambda e, n=nblk, bk=bank: e.activation(
                stg4[0:n, :], PS[bk][0:n, SOF + 128:SOF + 256], AF.Identity),
                deps=[tr2] + pr["stg"], tiny=True)
            dms = []
            for i4 in range(nblk):
                wseg = min(128, wpad - i4 * 128)
                dms.append(S.add("gpsimd", lambda e, ro=rowoff, i=i4, ww_=wseg:
                                 e.dma_start(
                    out=rowb[0:1, ro + i * 128: ro + i * 128 + ww_],
                    in_=stg4[i:i + 1, 0:ww_]), deps=[ev], dma=True))
            pr["stg"] = dms
            pr["ps67"] = [rds[-1], ev]
            return S.barrier("gpsimd", dms), ev

        def bcast(rowoff, wlen, dstb, deps, prevr):
            bank = 6 + (ps67_rr[0] % 2)
            ps67_rr[0] += 1
            mm1 = S.add("tensor", lambda e, r_=rowoff, n=wlen, bk=bank: e.matmul(
                PS[bk][:, SOF:SOF + n], ones1_f[:], rowb[0:1, r_:r_ + n],
                start=True, stop=True),
                deps=list(deps) + [c_o1f] + pr["ps67"])
            ev = S.add("scalar", lambda e, d=dstb, n=wlen, bk=bank: e.activation(
                d[:, 0:n], PS[bk][:, SOF:SOF + n], AF.Identity),
                deps=[mm1] + list(prevr))
            pr["ps67"] = [ev]
            return ev

        def ff_head(ti):
            t0, tw = FF_TILES[ti]
            ww = tw + 2
            p = ti % 2
            xlb = xload[p]
            x1bp = x1b[p]
            s = {"p": p}
            TS[ti] = s
            xl_ops = []
            if ti <= 1 or tw < TTF:
                xl_ops.append(S.add("vector", lambda e, b=xlb: e.memset(b[:], 0.0),
                                    deps=pr["xl"][p]))
            if ti == 0:
                lo_eff, off_eff, n_eff = 0, 1, ww - 1
            elif t0 + tw >= T:
                lo_eff, off_eff, n_eff = t0 - 1, 0, ww - 1
            else:
                lo_eff, off_eff, n_eff = t0 - 1, 0, ww
            pieces = []
            if lo_eff < ATT0:
                n_a = min(n_eff, ATT0 - lo_eff)
                pieces.append(("x", lo_eff, off_eff, n_a))
                if n_eff > n_a:
                    pieces.append(("a", lo_eff + n_a, off_eff + n_a, n_eff - n_a))
            else:
                pieces.append(("a", lo_eff, off_eff, n_eff))
            base_deps = list(xl_ops) if xl_ops else list(pr["xl"][p])
            for kind, src0, dsto, n in pieces:
                if kind == "x":
                    ap = dram_ap(xTh, src0, [[T, 128], [128 * T, 8], [1, n]])
                else:
                    ap = dram_ap(xattnh, src0 - ATT0,
                                 [[QT, 128], [128 * QT, 8], [1, n]])
                dstp = bass.AP(xlb, dsto, [[8 * W, 128], [W, 8], [1, n]])
                slow = n < 16
                xl_ops.append(S.add("sync", lambda e, a=ap, dd=dstp, sl=slow:
                                    e.dma_start(out=dd, in_=a,
                                                allow_slow_non_contiguous=sl),
                                    deps=base_deps + [xat_dma], dma=True))
            yield
            x1_ops = []
            cb_ops = []
            sq_ops = []
            for dc in range(NCH):
                x1_ops.append(S.add("scalar", lambda e, b=xlb, c=dc, n=ww:
                                    e.activation(
                    b[:, c * W:c * W + n], b[:, c * W:c * W + n],
                    AF.Identity, bias=pjb_s[:, c:c + 1]),
                    deps=xl_ops + [ld_pjb]))
                cb_ops.append(S.add("vector", lambda e, b=xlb, bb=x1bp, c=dc, n=ww:
                                    e.tensor_copy(
                    bb[:, c * W:c * W + n], b[:, c * W:c * W + n]),
                    deps=[x1_ops[dc]] + pr["x1b"][p]))
                sq_ops.append(S.add("scalar", lambda e, b=xlb, c=dc, n=ww:
                                    e.activation(
                    xqb[:, c * W:c * W + n], b[:, c * W:c * W + n], AF.Square),
                    deps=[x1_ops[dc]] + pr["xqb"]))
                if dc in (2, 5):
                    yield
            s["x1_ops"] = x1_ops
            yield
            st_mm = []
            sq_mm = []
            for dc in range(NCH):
                st_mm.append(S.add("tensor", lambda e, bb=x1bp, c=dc, n=ww:
                                   e.matmul(
                    PS[4][0:1, SOF:SOF + n], onescol_b[:], bb[:, c * W:c * W + n],
                    start=(c == 0), stop=(c == NCH - 1)),
                    deps=[cb_ops[dc], c_on_b] + pr["ps45"]))
                sq_mm.append(S.add("tensor", lambda e, c=dc, n=ww: e.matmul(
                    PS[5][0:1, SOF:SOF + n], onescol_b[:], xqb[:, c * W:c * W + n],
                    start=(c == 0), stop=(c == NCH - 1)),
                    deps=[sq_ops[dc]] + pr["ps45"]))
                if dc == 4:
                    yield
            s["st_mm"] = st_mm
            pr["xqb"] = [sq_mm[-1]]
            yield
            r_mu = S.add("scalar", lambda e, n=ww: e.activation(
                rowb[0:1, H_MU:H_MU + n], PS[4][0:1, SOF:SOF + n], AF.Identity,
                scale=1.0 / D), deps=[st_mm[-1]] + pr["rows_h"])
            r_ms = S.add("scalar", lambda e, n=ww: e.activation(
                rowb[0:1, H_MSQ:H_MSQ + n], PS[5][0:1, SOF:SOF + n], AF.Identity,
                scale=1.0 / D), deps=[sq_mm[-1]] + pr["rows_h"])
            pr["ps45"] = [r_mu, r_ms]
            r_v1 = S.add("vector", lambda e, n=ww: e.tensor_mul(
                rowb[0:1, H_VAR:H_VAR + n], rowb[0:1, H_MU:H_MU + n],
                rowb[0:1, H_MU:H_MU + n]), deps=[r_mu] + pr["rows_h"])
            r_v2 = S.add("vector", lambda e, n=ww: e.tensor_sub(
                rowb[0:1, H_VAR:H_VAR + n], rowb[0:1, H_MSQ:H_MSQ + n],
                rowb[0:1, H_VAR:H_VAR + n]), deps=[r_ms, r_v1])
            r_sd = S.add("scalar", lambda e, n=ww: e.activation(
                rowb[0:1, H_RSTD:H_RSTD + n], rowb[0:1, H_VAR:H_VAR + n],
                AF.Sqrt, bias=epsc[0:1, 0:1]), deps=[r_v2, c_eps])
            r_rs = S.add("vector", lambda e, n=ww: e.reciprocal(
                rowb[0:1, H_RSTD:H_RSTD + n], rowb[0:1, H_RSTD:H_RSTD + n]),
                deps=[r_sd])
            yield
            zp1 = S.add("vector", lambda e: e.memset(mxb[:, 0:256], -BIG),
                        deps=pr["mx"])
            zp2 = S.add("vector", lambda e: e.memset(mnb[:, 0:256], BIG),
                        deps=pr["mn"])
            x13 = bass.AP(xlb, 0, [[8 * W, 128], [1, ww], [W, 8]])
            fmx = S.add("vector", lambda e, a=x13, n=ww: e.tensor_reduce(
                mxb[:, 0:n], a, axis=AX, op=ALU.max), deps=x1_ops + [zp1])
            fmn = S.add("vector", lambda e, a=x13, n=ww: e.tensor_reduce(
                mnb[:, 0:n], a, axis=AX, op=ALU.min), deps=x1_ops + [zp2])
            yield
            mx_dma, mx_ev = cmax(mxb, [fmx], False, H_MX, 256)
            pr["mx"] = [mx_ev]
            yield
            mn_dma, mn_ev = cmax(mnb, [fmn], True, H_MN, 256)
            pr["mn"] = [mn_ev]
            yield
            q1 = S.add("vector", lambda e, n=ww: e.tensor_sub(
                rowb[0:1, H_MX:H_MX + n], rowb[0:1, H_MX:H_MX + n],
                rowb[0:1, H_MU:H_MU + n]), deps=[mx_dma, r_v1])
            q2 = S.add("vector", lambda e, n=ww: e.tensor_sub(
                rowb[0:1, H_MN:H_MN + n], rowb[0:1, H_MU:H_MU + n],
                rowb[0:1, H_MN:H_MN + n]), deps=[mn_dma, q1])
            q3 = S.add("vector", lambda e, n=ww: e.tensor_max(
                rowb[0:1, H_MX:H_MX + n], rowb[0:1, H_MX:H_MX + n],
                rowb[0:1, H_MN:H_MN + n]), deps=[q2])
            q4 = S.add("vector", lambda e, n=ww: e.tensor_mul(
                rowb[0:1, H_MX:H_MX + n], rowb[0:1, H_MX:H_MX + n],
                rowb[0:1, H_RSTD:H_RSTD + n]), deps=[q3, r_rs])
            q5 = S.add("vector", lambda e, n=ww: e.tensor_scalar(
                rowb[0:1, H_MX:H_MX + n], rowb[0:1, H_MX:H_MX + n],
                1e-5, None, ALU.max), deps=[q4])
            q6 = S.add("vector", lambda e, n=ww: e.reciprocal(
                rowb[0:1, H_MN:H_MN + n], rowb[0:1, H_MX:H_MX + n]), deps=[q5])
            q7 = S.add("vector", lambda e, n=ww: e.tensor_scalar(
                rowb[0:1, H_MN:H_MN + n], rowb[0:1, H_MN:H_MN + n],
                127.0, None, ALU.mult), deps=[q6])
            q8 = S.add("vector", lambda e, n=ww: e.tensor_scalar(
                rowb[0:1, H_MX:H_MX + n], rowb[0:1, H_MX:H_MX + n],
                c1_s[0:1, 0:1], 1.0 / 127.0, ALU.mult, ALU.mult),
                deps=[q6, c1_op])
            q9 = S.add("vector", lambda e, n=ww: e.tensor_mul(
                rowb[0:1, H_MSQ:H_MSQ + n], rowb[0:1, H_RSTD:H_RSTD + n],
                rowb[0:1, H_MN:H_MN + n]), deps=[q7, r_v2])
            q10 = S.add("vector", lambda e, n=ww: e.tensor_mul(
                rowb[0:1, H_MU:H_MU + n], rowb[0:1, H_MU:H_MU + n],
                rowb[0:1, H_MSQ:H_MSQ + n]), deps=[q9, r_v1])
            q11 = S.add("vector", lambda e, n=ww: e.tensor_scalar(
                rowb[0:1, H_MU:H_MU + n], rowb[0:1, H_MU:H_MU + n],
                -1.0, None, ALU.mult), deps=[q10])
            yield
            bA = bcast(H_MSQ, ww, bcA, [q9], pr["bcH"])
            bB = bcast(H_MU, ww, bcB, [q11], pr["bcH"])
            bC = bcast(H_MX, ww, bcC, [q8], pr["bcH"])
            pr["rows_h"] = [q11, bC]
            yield
            n1_ops = []
            for dc in range(NCH):
                a1 = S.add("vector", lambda e, b=xlb, c=dc, n=ww: e.tensor_mul(
                    x1c[:, 0:n], b[:, c * W:c * W + n], bcA[:, 0:n]),
                    deps=[bA, x1_ops[dc], pr["ch"]])
                a2 = S.add("vector", lambda e, n=ww: e.tensor_add(
                    x1c[:, 0:n], x1c[:, 0:n], bcB[:, 0:n]), deps=[a1, bB])
                a3 = S.add("vector", lambda e, n=ww: e.tensor_scalar(
                    x1c[:, 0:n], x1c[:, 0:n], Ca[:], Ca[:],
                    ALU.add, ALU.subtract), deps=[a2, Ca_ready])
                a4 = S.add("vector", lambda e, bb=x1bp, c=dc, n=ww: e.tensor_mul(
                    bb[:, c * W:c * W + n], x1c[:, 0:n], bcC[:, 0:n]),
                    deps=[a3, bC, st_mm[dc]] + pr["x1b"][p])
                pr["ch"] = a4
                n1_ops.append(a4)
                if dc in (1, 3, 5):
                    yield
            s["n1_ops"] = n1_ops
            pr["bcH"] = [n1_ops[-1]]
            pr["xl"][p] = [cb_ops[-1], sq_ops[-1], fmx, fmn, n1_ops[-1]]

        def pump(g):
            if g is None:
                return
            try:
                next(g)
            except StopIteration:
                pass

        def drain(g):
            if g is None:
                return
            for _ in g:
                pass

        def ff_mid(ti, g):
            t0, tw = FF_TILES[ti]
            ww = tw + 2
            s = TS[ti]
            p = s["p"]
            n1p = n1[p]
            n1_ops = s["n1_ops"]
            if ti == 0:
                hoff, hN = 1, ww - 1
            elif t0 + tw >= T:
                hoff, hN = 0, ww - 1
            else:
                hoff, hN = 0, ww
            conv_ops = []
            hfree = [None] * 4
            tcprev = {0: pr["tcab"], 1: pr["tcab"]}
            w1_last = None
            for cc in range(32):
                slot = cc % 4
                tp = cc % 2
                mms = []
                for dc in range(NCH):
                    mms.append(S.add("tensor",
                                     lambda e, b=n1p, c=dc, k=cc, sl=slot,
                                     ho=hoff, n=hN: e.matmul(
                        PS[sl][:, ho:ho + n],
                        w1q[:, c * HID + k * 128: c * HID + (k + 1) * 128],
                        b[:, c * W + ho: c * W + ho + n],
                        start=(c == 0), stop=(c == NCH - 1)),
                        deps=[n1_ops[dc], w1q_ops[-1], hfree[slot],
                              pr["psy"][slot]]))
                zops = []
                if ti == 0:
                    zops.append(S.add("vector", lambda e, sl=slot: e.memset(
                        PS[sl][:, 0:1], 0.0), deps=[mms[-1]], tiny=True))
                if t0 + tw >= T:
                    zops.append(S.add("vector", lambda e, sl=slot, n=ww: e.memset(
                        PS[sl][:, n - 1:n], 0.0), deps=[mms[-1]], tiny=True))
                c1o = S.add("scalar", lambda e, sl=slot, k=cc, n=tw, q=tp:
                            e.activation(
                    tca[q][:, 0:n], PS[sl][:, 2:2 + n], AF.Identity,
                    scale=cw_s[:, 64 + k:65 + k]),
                    deps=mms + zops + [ld_cw] + list(tcprev[tp]))
                c2o = S.add("vector", lambda e, sl=slot, k=cc, n=tw, q=tp:
                            e.scalar_tensor_tensor(
                    tcb[:, 0:n], PS[sl][:, 1:1 + n], cw_s[:, 32 + k:33 + k],
                    tca[q][:, 0:n], ALU.mult, ALU.add), deps=[c1o])
                c3o = S.add("vector", lambda e, sl=slot, k=cc, n=tw:
                            e.scalar_tensor_tensor(
                    u_all[:, k * TTF:k * TTF + n], PS[sl][:, 0:n],
                    cw_s[:, k:k + 1], tcb[:, 0:n], ALU.mult, ALU.add),
                    deps=[c2o])
                hfree[slot] = c3o
                tcprev[tp] = [c2o]
                si = S.add("scalar", lambda e, k=cc, n=tw: e.activation(
                    u_all[:, k * TTF:k * TTF + n], u_all[:, k * TTF:k * TTF + n],
                    AF.Silu, bias=cb_s[:, k:k + 1]), deps=[c3o, ld_cb])
                conv_ops.append(si)
                w1_last = mms[-1]
                pump(g)
            s["conv_ops"] = conv_ops
            pr["x1b"][p] = [w1_last]

        def ff_tail(ti):
            t0, tw = FF_TILES[ti]
            s = TS[ti]
            p = s["p"]
            conv_ops = s["conv_ops"]
            zp = S.add("vector", lambda e: e.memset(m2b[:, 0:256], 0.0),
                       deps=pr["m2"])
            u3 = bass.AP(u_all, 0, [[32 * TTF, 128], [1, tw], [TTF, 32]])
            f2 = S.add("vector", lambda e, a=u3, n=tw: e.tensor_reduce(
                m2b[:, 0:n], a, axis=AX, op=ALU.max, apply_absolute_value=True),
                deps=conv_ops + [zp])
            m2_dma, m2_ev = cmax(m2b, [f2], False, T_S2, 256)
            pr["m2"] = [m2_ev]
            w1r = S.add("vector", lambda e, n=tw: e.tensor_scalar(
                rowb[0:1, T_S2:T_S2 + n], rowb[0:1, T_S2:T_S2 + n],
                1e-5, None, ALU.max), deps=[m2_dma] + pr["rows_t"])
            w2r = S.add("vector", lambda e, n=tw: e.reciprocal(
                rowb[0:1, T_I2:T_I2 + n], rowb[0:1, T_S2:T_S2 + n]),
                deps=[w1r] + pr["rows_t"])
            w3r = S.add("vector", lambda e, n=tw: e.tensor_scalar(
                rowb[0:1, T_I2:T_I2 + n], rowb[0:1, T_I2:T_I2 + n],
                127.0, None, ALU.mult), deps=[w2r])
            w4r = S.add("vector", lambda e, n=tw: e.tensor_scalar(
                rowb[0:1, T_S2:T_S2 + n], rowb[0:1, T_S2:T_S2 + n],
                c2_s[0:1, 0:1], 1.0 / 127.0, ALU.mult, ALU.mult),
                deps=[w2r, c2_op])
            bS = bcast(T_I2, tw, bcS, [w3r], pr["bcT"])
            bI = bcast(T_S2, tw, bcI, [w4r], pr["bcT"])
            pr["rows_t"] = [bS, bI]
            y_mms = [[] for _ in range(8)]
            for g2 in range(16):
                n2o2 = []
                for k2 in range(2):
                    cc = g2 * 2 + k2
                    uqo = S.add("vector", lambda e, k=cc, n=tw: e.tensor_mul(
                        tcb[:, 0:n], u_all[:, k * TTF:k * TTF + n], bcS[:, 0:n]),
                        deps=[conv_ops[cc], bS])
                    n2o = S.add("vector", lambda e, k2_=k2, n=tw: e.tensor_scalar(
                        n2g[:, k2_ * TTF:k2_ * TTF + n], tcb[:, 0:n],
                        Ca[:], Ca[:], ALU.add, ALU.subtract),
                        deps=[uqo, Ca_ready, pr["n2g"][k2]])
                    n2o2.append(n2o)
                for k2 in range(2):
                    cc = g2 * 2 + k2
                    for oc in range(NCH):
                        dd = [n2o2[k2], w2q_ops[-1]]
                        if cc == 0:
                            dd += [conv_ops[-1], bI, pr["psy"][oc]]
                        y_mms[oc].append(S.add("tensor",
                                               lambda e, k=cc, o=oc, k2_=k2,
                                               f=(cc == 0), n=tw: e.matmul(
                            PS[o][:, 0:n],
                            w2q[:, k * D + o * 128:k * D + (o + 1) * 128],
                            n2g[:, k2_ * TTF:k2_ * TTF + n],
                            start=f, stop=(k == 31)), deps=dd))
                    pr["n2g"][k2] = y_mms[7][-1]
            ty_oc = {}
            oo_ops = []
            for oc in range(NCH):
                pp = oc % 2
                ty = S.add("vector", lambda e, o=oc, n=tw, q=pp: e.tensor_mul(
                    tca[q][:, 0:n], PS[o][:, 0:n], bcI[:, 0:n]),
                    deps=[y_mms[oc][-1], bI])
                ty_oc[oc] = ty
                oo = S.add("vector", lambda e, b=xload[p], o=oc, n=tw, q=pp:
                           e.tensor_add(
                    osb[q][:, 0:n], tca[q][:, 0:n],
                    b[:, o * W + 1:o * W + 1 + n]),
                    deps=[ty, s["x1_ops"][oc], pr["osb"][pp]])
                oo_ops.append(oo)
                oap = dram_ap(outTh, oc * 128 * T + t0, [[T, 128], [1, tw]])
                dm = S.add("sync", lambda e, a=oap, n=tw, q=pp: e.dma_start(
                    out=a, in_=osb[q][:, 0:n]), deps=[oo], dma=True)
                pr["osb"][pp] = dm
                out_dmas.append(dm)
            pr["xl"][p] = pr["xl"][p] + oo_ops
            pr["psy"] = [ty_oc[oc] for oc in range(8)]
            pr["bcT"] = [oo_ops[-1]]
            pr["tcab"] = [oo_ops[-1]]

        g = ff_head(0)
        drain(g)
        for ti in range(NT):
            g = ff_head(ti + 1) if ti + 1 < NT else None
            ff_mid(ti, g)
            drain(g)
            ff_tail(ti)
        S.barrier("sync", out_dmas)

        stF.close()
        S.emit(nc, sems)
    return nc, dbg


def prep_inputs(L0_attn_x, L0_attn_qkv_w, L0_attn_qkv_b, L0_attn_proj_w,
                L0_attn_proj_b, L0_ffn_w1, L0_ffn_conv_w, L0_ffn_conv_b,
                L0_ffn_w2):
    x = np.asarray(L0_attn_x, dtype=np.float32)
    qkv_w = np.asarray(L0_attn_qkv_w, dtype=np.float32)
    qkv_b = np.asarray(L0_attn_qkv_b, dtype=np.float32)
    proj_w = np.asarray(L0_attn_proj_w, dtype=np.float32)
    proj_b = np.asarray(L0_attn_proj_b, dtype=np.float32)
    w1 = np.asarray(L0_ffn_w1, dtype=np.float32)
    conv_w = np.asarray(L0_ffn_conv_w, dtype=np.float32)
    conv_b = np.asarray(L0_ffn_conv_b, dtype=np.float32)
    w2 = np.asarray(L0_ffn_w2, dtype=np.float32)

    shared = {
        "wqT": np.ascontiguousarray(qkv_w[0:D].T),
        "wkT": np.ascontiguousarray(qkv_w[D:2 * D].T),
        "wvT": np.ascontiguousarray(qkv_w[2 * D:3 * D].T),
        "qkb": np.ascontiguousarray(qkv_b[0:2 * D].reshape(16, 128).T),
        "vbrow": np.ascontiguousarray(qkv_b[2 * D:3 * D].reshape(1, D)),
        "projT": np.ascontiguousarray(proj_w.T),
        "pjb": np.ascontiguousarray(proj_b.reshape(8, 128).T),
        "w1T": np.ascontiguousarray(w1.T),
        "cw": np.ascontiguousarray(
            conv_w.reshape(32, 128, 3).transpose(1, 2, 0).reshape(128, 96)),
        "cbi": np.ascontiguousarray(conv_b.reshape(32, 128).T),
        "w2T": np.ascontiguousarray(w2.T),
    }
    in_maps = []
    for b in range(x.shape[0]):
        m = dict(shared)
        m["xT"] = np.ascontiguousarray(x[b].T)
        in_maps.append(m)
    return in_maps


def kernel(**inputs):
    in_maps = prep_inputs(**inputs)
    nc, _ = build()
    res = run_bass_kernel_spmd(nc, in_maps, core_ids=list(range(8)))
    out = np.empty((8, T, D), dtype=np.float32)
    for b in range(8):
        out[b] = np.asarray(res.results[b]["outT"]).T
    return out


if __name__ == "__main__":
    build()
    print("build ok")
